# version 06 (final)
# baseline (speedup 1.0000x reference)
import contextlib
import numpy as np
import concourse.bass as bass
import concourse.mybir as mybir
from concourse.bass_utils import run_bass_kernel_spmd

F32 = mybir.dt.float32
BF16 = mybir.dt.bfloat16
I16 = mybir.dt.int16
I32 = mybir.dt.int32
AF = mybir.ActivationFunctionType
ALU = mybir.AluOpType
AXX = mybir.AxisListType.X

NCORES = 8
B, T, C = 4, 2048, 2048
TL = 1024
TT = TL // 128
CT = C // 128
H, DH, HKV = 16, 128, 4
HF = 8192
HFT = HF // 128

EPS_LN = 1e-5
EPS_RMS = 1e-6
EPS_FFN = 1e-6
EPS_W = 1e-5
QMAX = 127.0
RT128 = float(1.0 / np.sqrt(128.0))
EXP_BIAS = -11.3138

OFF_X = 20480
OFF_SCRA = 86016
OFF_XLNT = 192512
OFF_MISC = 225280
SCRA_SZ = OFF_XLNT - OFF_SCRA

N_SUBLAYERS = 6
RUN_LIST = None
DEBUG_DUMP = None
L1FFN_STOP = 99
KVE = HKV * DH * TL + TL * HKV * DH


class Ctr:
    def __init__(self, es, nc, name):
        self.h = es.enter_context(nc.semaphore(name))
        self.n = 0

    def inc(self, instr, k=1):
        instr.then_inc(self.h, k)
        self.n += k
        return self.n


class P:
    def __init__(self, nc, es, block):
        self.nc = nc
        self.es = es
        self.block = block
        self._sems = {}
        self.t = {}
        self.s = {}
        self._uid = 0
        self.cast_done = {}

    def sem(self, name):
        if name not in self._sems:
            self._sems[name] = Ctr(self.es, self.nc, name)
        return self._sems[name]

    def uid(self):
        self._uid += 1
        return self._uid

    def at(self, name, shape, dtype, abs_off):
        nb = int(np.prod(shape[1:])) * mybir.dt.size(dtype) if len(shape) > 1 \
            else mybir.dt.size(dtype)
        end = abs_off + nb
        assert abs_off >= OFF_X and end <= 229376, (name, abs_off, end)
        return self.nc.alloc_sbuf_tensor_at(
            f"{name}_{self.uid()}", shape, dtype, offset=abs_off)

    def vfence(self, v, instr):
        sem = self.sem("vsy")
        sem.n += 1
        instr.then_inc(sem.h, 1)
        v.wait_ge(sem.h, sem.n)

    def scra(self, name, shape, dtype, off):
        nb = int(np.prod(shape[1:])) * mybir.dt.size(dtype) if len(shape) > 1 \
            else mybir.dt.size(dtype)
        assert off + nb <= SCRA_SZ, (name, off, nb)
        return self.at(name, shape, dtype, OFF_SCRA + off)


def emit_ln_to_xlnt(p, x_ready, tag, quant=False):
    nc, block = p.nc, p.block
    vv, ss, tv = p.sem("vv"), p.sem("ss"), p.sem("tv")
    X, XLNT = p.s["X"], p.s["XLNT"]
    MU, RS, NB, S1 = p.s["MU"], p.s["RS"], p.s["NB"], p.s["S1"]
    BNS, MV, IDN = p.s["BNS"], p.s["MV"], p.s["IDN"]
    xt_f = p.scra(f"lnx{tag}", [128, 2 * 2048], F32, 65536)
    k1s = [p.scra(f"lnk{tag}a", [128, 2048], I16, 81920),
           p.scra(f"lnk{tag}b", [128, 2048], I16, 86016)] if quant else None
    xt_b = p.scra(f"lnb{tag}", [128, 2 * 2048], BF16, 90112)

    ssq0, vsq0 = ss.n, vv.n
    @block.vector
    def _(v):
        v.wait_ge(x_ready[0].h, x_ready[1])
        for m in range(TT):
            v.reduce_sum(MU[:, m:m + 1], X[:, m * C:(m + 1) * C], axis=AXX)
            isq = v.tensor_tensor(xt_f[:, 0:2048], X[:, m * C:(m + 1) * C],
                                  X[:, m * C:(m + 1) * C], ALU.mult)
            p.vfence(v, isq)
            iv = v.reduce_sum(RS[:, m:m + 1],
                              xt_f[:, 0:2048],
                              axis=AXX)
            vv.n += 1
            iv.then_inc(vv.h, 1)
        v.wait_ge(vv.h, vv.n)
        i1 = v.tensor_single_scalar(MU[:, 0:TT], MU[:, 0:TT], 1.0 / C, ALU.mult)
        i2 = v.tensor_single_scalar(RS[:, 0:TT], RS[:, 0:TT], 1.0 / C, ALU.mult)
        p.vfence(v, i2)
        i3 = v.tensor_tensor(NB[:, 0:TT], MU[:, 0:TT], MU[:, 0:TT], ALU.mult)
        p.vfence(v, i3)
        i4 = v.tensor_tensor(RS[:, 0:TT], RS[:, 0:TT], NB[:, 0:TT], ALU.subtract)
        p.vfence(v, i4)
        iv2 = v.tensor_single_scalar(RS[:, 0:TT], RS[:, 0:TT], EPS_LN, ALU.add)
        vv.n += 1
        iv2.then_inc(vv.h, 1)
    stats_done = vv.n

    @block.scalar
    def _(s):
        s.wait_ge(vv.h, stats_done)
        s.activation(RS[:, 0:TT], RS[:, 0:TT], AF.Sqrt).then_inc(ss.h, 1)
    ss.n += 1
    sqrt_done = ss.n

    @block.vector
    def _(v):
        v.wait_ge(ss.h, sqrt_done)
        i1 = v.reciprocal(RS[:, 0:TT], RS[:, 0:TT])
        p.vfence(v, i1)
        i2 = v.tensor_tensor(NB[:, 0:TT], MU[:, 0:TT], RS[:, 0:TT], ALU.mult)
        p.vfence(v, i2)
        v.tensor_single_scalar(NB[:, 0:TT], NB[:, 0:TT], -1.0,
                               ALU.mult).then_inc(vv.h, 1)
    vv.n += 1
    rs_done = vv.n

    ssb = ss.n
    vvb = vv.n
    tvb = tv.n
    evb = None
    ev = p.sem("ev")
    evbase = ev.n

    @block.scalar
    def _(s):
        s.wait_ge(vv.h, rs_done)
        for m in range(TT):
            if m >= 2:
                if quant:
                    s.wait_ge(vv.h, vvb + (m - 1))
                else:
                    s.wait_ge(tv.h, tvb + 4 * (m - 1))
            sl = (m % 2)
            i2 = s.activation(xt_f[:, sl * 2048:(sl + 1) * 2048] if quant
                              else xt_b[:, sl * 2048:(sl + 1) * 2048],
                              X[:, m * C:(m + 1) * C],
                              AF.Identity, bias=NB[:, m:m + 1], scale=RS[:, m:m + 1])
            ss.n += 1
            i2.then_inc(ss.h, 1)

    with (nc.psum_tensor(f"trp{tag}a", [128, 512], BF16) as pa,
          nc.psum_tensor(f"trp{tag}b", [128, 512], BF16) as pb):
        pss = [pa, pb]

        def emit_quant(v, m):
            sl = m % 2
            v.wait_ge(ss.h, ssb + m + 1)
            if m >= 2:
                v.wait_ge(tv.h, tvb + 4 * (m - 1))
            xf = xt_f[:, sl * 2048:(sl + 1) * 2048]
            i1 = v.reduce_max(S1[:, 8 + m:9 + m], xf, axis=AXX,
                              apply_absolute_value=True)
            p.vfence(v, i1)
            i2 = v.tensor_scalar_max(S1[:, 8 + m:9 + m], S1[:, 8 + m:9 + m],
                                     EPS_W)
            p.vfence(v, i2)
            i3 = v.reciprocal(S1[:, m:m + 1], S1[:, 8 + m:9 + m])
            p.vfence(v, i3)
            i4 = v.tensor_single_scalar(S1[:, m:m + 1], S1[:, m:m + 1],
                                        QMAX, ALU.mult)
            p.vfence(v, i4)
            i5 = v.tensor_scalar(k1s[sl][:], xf, S1[:, m:m + 1], 128.0,
                                 ALU.mult, ALU.add)
            p.vfence(v, i5)
            iq = v.tensor_single_scalar(xt_b[:, sl * 2048:(sl + 1) * 2048],
                                        k1s[sl][:], 128, ALU.subtract)
            vv.n += 1
            iq.then_inc(vv.h, 1)

        def emit_evac(v, m):
            for g in range(4):
                idx = m * 4 + g
                v.wait_ge(tv.h, tvb + idx + 1)
                ps = pss[idx % 2]
                src = ps[:].rearrange("p (j f) -> p j f", f=128)
                dst = XLNT[:].rearrange("p (k f) -> p k f", f=TL)[
                    :, g * 4:(g + 1) * 4, m * 128:(m + 1) * 128]
                ie = v.tensor_copy(dst, src)
                ev.n += 1
                ie.then_inc(ev.h, 1)

        @block.vector
        def _(v):
            for m in range(TT):
                if quant:
                    emit_quant(v, m)
                if m >= 1:
                    emit_evac(v, m - 1)
            emit_evac(v, TT - 1)

        @block.tensor
        def _(t):
            for m in range(TT):
                sl = m % 2
                if quant:
                    t.wait_ge(vv.h, vvb + m + 1)
                else:
                    t.wait_ge(ss.h, ssb + m + 1)
                for g in range(4):
                    idx = m * 4 + g
                    if idx >= 2:
                        t.wait_ge(ev.h, evbase + idx - 1)
                    ps = pss[idx % 2]
                    for j in range(4):
                        kc = g * 4 + j
                        it = t.transpose(
                            ps[:, j * 128:(j + 1) * 128],
                            xt_b[:, sl * 2048 + kc * 128: sl * 2048 + (kc + 1) * 128],
                            IDN[:])
                    tv.n += 1
                    it.then_inc(tv.h, 1)
    return (ev, ev.n)


def emit_wstream_mm(p, tag, w_dram, n_cols, lhsT_fn, evac_fn, ready,
                    kc_count=CT, m_count=TT, t_ready=None, ready2=None):
    nc, block = p.nc, p.block
    sq, tv, ev = p.sem("sq"), p.sem("tv"), p.sem("ev")
    NCH = n_cols // 512
    wb = p.scra(f"wb{tag}", [128, 2 * 512], BF16, 104448)
    sqb, tvb, evb = sq.n, tv.n, ev.n
    with contextlib.ExitStack() as pes:
        banks = [pes.enter_context(
            nc.psum_tensor(f"mmp{tag}{i}", [128, 512], F32)) for i in range(8)]

        @block.sync
        def _(sy):
            sy.wait_ge(ready[0].h, ready[1])
            if ready2 is not None:
                sy.wait_ge(ready2[0].h, ready2[1])
            for nch in range(NCH):
                for kc in range(kc_count):
                    idx = nch * kc_count + kc
                    if idx >= 2:
                        sy.wait_ge(tv.h, tvb + idx - 1)
                    d = sy.dma_start(
                        out=wb[:, (idx % 2) * 512:(idx % 2 + 1) * 512],
                        in_=w_dram[kc * 128:(kc + 1) * 128,
                                   nch * 512:(nch + 1) * 512])
                    sq.n += 16
                    d.then_inc(sq.h, 16)

        @block.tensor
        def _(t):
            if t_ready is not None:
                t.wait_ge(t_ready[0].h, t_ready[1])
            for nch in range(NCH):
                for kc in range(kc_count):
                    idx = nch * kc_count + kc
                    t.wait_ge(sq.h, sqb + 16 * (idx + 1))
                    if kc == 0 and nch > 0:
                        t.wait_ge(ev.h, evb + nch * m_count)
                    wap = wb[:, (idx % 2) * 512:(idx % 2 + 1) * 512]
                    for m in range(m_count):
                        it = t.matmul(banks[m][:], lhsT_fn(kc, m), wap,
                                      start=(kc == 0), stop=(kc == kc_count - 1))
                    tv.n += 1
                    it.then_inc(tv.h, 1)

        for nch in range(NCH):
            mm_done = tvb + (nch + 1) * kc_count
            for m in range(m_count):
                evac_fn(nch, m, banks[m], (tv, mm_done), ev)
        assert ev.n == evb + NCH * m_count, (tag, ev.n, evb, NCH, m_count)
    return (ev, ev.n)


def emit_cast(p, src, dst, rows, cols, ready=None):
    if dst in p.cast_done:
        return p.cast_done[dst]
    nc, block = p.nc, p.block
    sq, vv, gq = p.sem("sq"), p.sem("vv"), p.sem("gq")
    cf = p.scra(f"cf{dst}", [128, 2 * 512], F32, 98304)
    cb = p.scra(f"cb{dst}", [128, 2 * 512], BF16, 102400)
    tiles = [(r, c) for r in range(rows // 128) for c in range(cols // 512)]
    n = len(tiles)
    sqb, vvb, gqb = sq.n, vv.n, gq.n

    @block.sync
    def _(sy):
        if ready is not None:
            sy.wait_ge(ready[0].h, ready[1])
        sy.wait_ge(vv.h, vvb)
        for i, (r, c) in enumerate(tiles):
            if i >= 2:
                sy.wait_ge(vv.h, vvb + i - 1)
            d = sy.dma_start(out=cf[:, (i % 2) * 512:(i % 2 + 1) * 512],
                             in_=p.t[src].ap()[r * 128:(r + 1) * 128,
                                               c * 512:(c + 1) * 512])
            sq.n += 16
            d.then_inc(sq.h, 16)

    @block.vector
    def _(v):
        for i in range(n):
            v.wait_ge(sq.h, sqb + 16 * min(i + 2, n))
            if i >= 2:
                v.wait_ge(gq.h, gqb + 16 * (i - 1))
            ic = v.tensor_copy(cb[:, (i % 2) * 512:(i % 2 + 1) * 512],
                               cf[:, (i % 2) * 512:(i % 2 + 1) * 512])
            vv.n += 1
            ic.then_inc(vv.h, 1)

    @block.gpsimd
    def _(g):
        for i, (r, c) in enumerate(tiles):
            g.wait_ge(vv.h, vvb + i + 1)
            d = g.dma_start(out=p.t[dst].ap()[r * 128:(r + 1) * 128,
                                              c * 512:(c + 1) * 512],
                            in_=cb[:, (i % 2) * 512:(i % 2 + 1) * 512])
            gq.n += 16
            d.then_inc(gq.h, 16)

    p.cast_done[dst] = (gq, gq.n)
    return p.cast_done[dst]


def bcast_row(p, engine_body_dev, dram_name, dst, n, sem, col0=0):
    src = p.t[dram_name].ap()[0:1, :].broadcast_to((dst.shape[0], n))
    d = engine_body_dev.dma_start(out=dst[:, col0:col0 + n], in_=src)
    sem.n += 16
    d.then_inc(sem.h, 16)
    return sem.n


def emit_l1attn(p, tag, x_ready):
    nc, block = p.nc, p.block
    sq, gq, cc = p.sem("sq"), p.sem("gq"), p.sem("cc")
    vv, ss, tv, ev = p.sem("vv"), p.sem("ss"), p.sem("tv"), p.sem("ev")
    X, XLNT, IDN = p.s["X"], p.s["XLNT"], p.s["IDN"]
    STAT, BT = p.s["STAT"], p.s["BT"]

    cW = emit_cast(p, "L1W", "L1W_bf", C, 3 * C)
    xlnt_done = emit_ln_to_xlnt(p, x_ready, tag)

    qT = p.scra(f"qT{tag}", [128, 16 * TL], BF16, 0)
    kvt = p.scra(f"kvt{tag}", [128, 2 * TL], BF16, 32768)
    scr = p.scra(f"scr{tag}", [128, TL], F32, 36864)
    wcol = p.scra(f"wcol{tag}", [128, 2 * CT * 128], BF16, 45056)
    qsc = p.scra(f"qsc{tag}", [128, 2 * TL], BF16, 57344)

    jorder = list(range(16)) + [x for i in range(16) for x in (16 + i, 32 + i)]

    if DEBUG_DUMP == "l1qt":
        @block.gpsimd
        def _(g):
            g.wait_ge(xlnt_done[0].h, xlnt_done[1])
            d = g.dma_start(out=p.t["dbg"].ap()[:, 0:1024], in_=XLNT[:, 0:1024])
            gq.n += 16; d.then_inc(gq.h, 16)
            d = g.dma_start(out=p.t["dbg"].ap()[:, 1024:1032],
                            in_=p.s["MU"][:, 0:8])
            gq.n += 16; d.then_inc(gq.h, 16)
            d = g.dma_start(out=p.t["dbg"].ap()[:, 1032:1040],
                            in_=p.s["RS"][:, 0:8])
            gq.n += 16; d.then_inc(gq.h, 16)
            d = g.dma_start(out=p.t["dbg"].ap()[:, 1040:1072],
                            in_=p.s["QRS"][:, 64:96])
            gq.n += 16; d.then_inc(gq.h, 16)

    sqb = sq.n
    @block.sync
    def _(sy):
        d = sy.dma_start(out=BT[:], in_=p.t["L1bT"].ap())
        sq.n += 16
        d.then_inc(sq.h, 16)
    bt_done = sq.n

    with (nc.psum_tensor(f"qk{tag}a", [128, TL], F32) as ja,
          nc.psum_tensor(f"qk{tag}b", [128, TL], F32) as jb):
        pss = [ja, jb]
        sq0, tv0, ss0, vv0 = sq.n, tv.n, ss.n, vv.n

        @block.sync
        def _(sy):
            sy.wait_ge(cW[0].h, cW[1])
            for jj, j in enumerate(jorder):
                if jj >= 2:
                    sy.wait_ge(tv.h, tv0 + jj - 1)
                d = sy.dma_start(
                    out=wcol[:, (jj % 2) * CT * 128:(jj % 2 + 1) * CT * 128],
                    in_=p.t["L1W_bf"].ap()[:, j * 128:(j + 1) * 128]
                        .rearrange("(k p) f -> p k f", p=128))
                sq.n += 16
                d.then_inc(sq.h, 16)

        @block.tensor
        def _(t):
            t.wait_ge(xlnt_done[0].h, xlnt_done[1])
            for jj, j in enumerate(jorder):
                t.wait_ge(sq.h, sq0 + 16 * (jj + 1))
                if jj >= 2:
                    t.wait_ge(ss.h, ss0 + jj - 1)
                ps = pss[jj % 2]
                wj = wcol[:, (jj % 2) * CT * 128:(jj % 2 + 1) * CT * 128]
                for kc in range(CT):
                    for qc in range(2):
                        it = t.matmul(
                            ps[:, qc * 512:(qc + 1) * 512],
                            wj[:, kc * 128:(kc + 1) * 128],
                            XLNT[:, kc * TL + qc * 512: kc * TL + (qc + 1) * 512],
                            start=(kc == 0), stop=(kc == CT - 1))
                tv.n += 1
                it.then_inc(tv.h, 1)

        @block.scalar
        def _(s):
            s.wait_ge(sq.h, bt_done)
            for jj, j in enumerate(jorder):
                s.wait_ge(tv.h, tv0 + jj + 1)
                if jj >= 18:
                    s.wait_ge(vv.h, vv0 + jj)
                ps = pss[jj % 2]
                if j < 16:
                    dst = qT[:, j * TL:(j + 1) * TL]
                elif j < 32:
                    dst = kvt[:, 0:TL]
                else:
                    dst = kvt[:, TL:2 * TL]
                ic = s.activation(dst, ps[:], AF.Identity, bias=BT[:, j:j + 1])
                ss.n += 1
                ic.then_inc(ss.h, 1)

        @block.vector
        def _(v):
            for jj, j in enumerate(jorder):
                v.wait_ge(ss.h, ss0 + jj + 1)
                if j < 16:
                    src0 = src1 = qT[:, j * TL:(j + 1) * TL]
                    col = j
                elif j < 32:
                    src0 = src1 = kvt[:, 0:TL]
                    col = j
                else:
                    src0, src1 = kvt[:, 0:TL], kvt[:, TL:2 * TL]
                    col = j
                i0 = v.tensor_tensor(scr[:], src0, src1, ALU.mult)
                p.vfence(v, i0)
                iv = v.reduce_sum(STAT[:, col:col + 1], scr[:], axis=AXX)
                vv.n += 1
                iv.then_inc(vv.h, 1)
    stats_done = vv.n

    gq0, cc0 = gq.n, cc.n
    @block.gpsimd
    def _(g):
        g.wait_ge(vv.h, stats_done)
        d = g.dma_start(out=p.t["st_in"].ap().rearrange("o (q f) -> (o q) f", q=128),
                        in_=STAT[:, 0:48])
        gq.n += 16
        d.then_inc(gq.h, 16)
        g.wait_ge(gq.h, gq.n)
        icc = g.collective_compute(
            "AllGather", ALU.bypass,
            replica_groups=[[0, 1], [2, 3], [4, 5], [6, 7]],
            ins=[p.t["st_in"].ap().opt()], outs=[p.t["st_out"].ap().opt()])
        cc.n += 1
        icc.then_inc(cc.h, 1)
        g.wait_ge(cc.h, cc.n)
        d = g.dma_start(
            out=STAT[:, 48:144].rearrange("p (r f) -> p r f", r=2),
            in_=p.t["st_out"].ap().rearrange("r (q f) -> q r f", q=128))
        gq.n += 16
        d.then_inc(gq.h, 16)
    st_loaded = gq.n

    QRS = p.s["QRS"]
    @block.vector
    def _(v):
        v.wait_ge(gq.h, st_loaded)
        i0 = v.tensor_add(STAT[:, 0:48], STAT[:, 48:96], STAT[:, 96:144])
        p.vfence(v, i0)
        i1 = v.tensor_tensor(QRS[:, 0:16], STAT[:, 0:16], STAT[:, 16:32],
                             ALU.mult)
        vv.n += 1
        i1.then_inc(vv.h, 1)
    c1 = vv.n
    @block.scalar
    def _(s):
        s.wait_ge(vv.h, c1)
        i2 = s.activation(QRS[:, 0:16], QRS[:, 0:16], AF.Sqrt)
        ss.n += 1
        i2.then_inc(ss.h, 1)
    c2 = ss.n
    @block.vector
    def _(v):
        v.wait_ge(ss.h, c2)
        i0 = v.reciprocal(QRS[:, 0:16], QRS[:, 0:16])
        p.vfence(v, i0)
        iv = v.tensor_tensor(QRS[:, 0:16], QRS[:, 0:16], STAT[:, 32:48],
                             ALU.mult)
        vv.n += 1
        iv.then_inc(vv.h, 1)
    fcol_done = vv.n

    if DEBUG_DUMP == "l1stats":
        @block.gpsimd
        def _(g):
            g.wait_ge(vv.h, fcol_done)
            d = g.dma_start(out=p.t["dbg"].ap()[:, 0:144], in_=STAT[:, 0:144])
            gq.n += 16; d.then_inc(gq.h, 16)
            d = g.dma_start(out=p.t["dbg"].ap()[:, 144:160], in_=QRS[:, 0:16])
            gq.n += 16; d.then_inc(gq.h, 16)

    ss1, tv1, ev1 = ss.n, tv.n, ev.n
    @block.scalar
    def _(s):
        s.wait_ge(vv.h, fcol_done)
        for j in range(16):
            if j >= 2:
                s.wait_ge(tv.h, tv1 + 2 * (j - 1))
            ic = s.activation(qsc[:, (j % 2) * TL:(j % 2 + 1) * TL],
                              qT[:, j * TL:(j + 1) * TL],
                              AF.Identity, scale=QRS[:, j:j + 1])
            ss.n += 1
            ic.then_inc(ss.h, 1)
    with (nc.psum_tensor(f"tb{tag}a", [128, 512], BF16) as ta,
          nc.psum_tensor(f"tb{tag}b", [128, 512], BF16) as tb):
        tps = [ta, tb]
        @block.tensor
        def _(t):
            for j in range(16):
                t.wait_ge(ss.h, ss1 + j + 1)
                for g2 in range(2):
                    idx = j * 2 + g2
                    if idx >= 2:
                        t.wait_ge(ev.h, ev1 + idx - 1)
                    ps = tps[idx % 2]
                    for mi in range(4):
                        m = g2 * 4 + mi
                        it = t.transpose(
                            ps[:, mi * 128:(mi + 1) * 128],
                            qsc[:, (j % 2) * TL + m * 128:(j % 2) * TL + (m + 1) * 128],
                            IDN[:])
                    tv.n += 1
                    it.then_inc(tv.h, 1)
        @block.vector
        def _(v):
            for j in range(16):
                for g2 in range(2):
                    idx = j * 2 + g2
                    v.wait_ge(tv.h, tv1 + idx + 1)
                    ps = tps[idx % 2]
                    src = ps[:].rearrange("p (mi f) -> p mi f", f=128)
                    dst = X[:].rearrange("p (m c) -> p m c", c=C)[
                        :, g2 * 4:(g2 + 1) * 4, j * 128:(j + 1) * 128]
                    iv = v.tensor_tensor(dst, dst, src, ALU.add)
                    ev.n += 1
                    iv.then_inc(ev.h, 1)
    return (ev, ev.n)


def emit_l0ffn(p, tag, x_ready):
    nc, block = p.nc, p.block
    sq, gq, cc = p.sem("sq"), p.sem("gq"), p.sem("cc")
    vv, ss, tv, ev = p.sem("vv"), p.sem("ss"), p.sem("tv"), p.sem("ev")
    X, XLNT, IDN, ONESR = p.s["X"], p.s["XLNT"], p.s["IDN"], p.s["ONESR"]
    MU, RS, NB, S1 = p.s["MU"], p.s["RS"], p.s["NB"], p.s["S1"]

    cag = emit_cast(p, "agg_w", "agg_bf", C, C)
    cd1 = emit_cast(p, "d1s", "d1s_bf", C, 1024)
    cd2 = emit_cast(p, "d2s", "d2s_bf", 1024, C)
    cga = emit_cast(p, "gate_w", "gate_bf", C, C)
    cbc = emit_cast(p, "bc_w", "bc_bf", C, C)
    xlnt_done = emit_ln_to_xlnt(p, x_ready, tag)

    ROWB = p.scra(f"rw{tag}", [128, 4096], F32, 65536)
    BB4 = p.scra(f"bb{tag}", [4, 2048], F32, 65536)
    IB = p.scra(f"ib{tag}", [1, 2048], F32, 90112)
    TH = p.scra(f"thh{tag}", [4, 2048], F32, 73728)
    IG = p.scra(f"igg{tag}", [1, 2048], F32, 81920)
    TN = p.scra(f"tn{tag}", [4, 2048], F32, 81920)
    BCB = p.scra(f"bcb{tag}", [128, 2048], F32, 81920)
    XLT = p.scra(f"xlt{tag}", [128, 64], BF16, 90112)
    TNT = p.scra(f"tnt{tag}", [128, 64], BF16, 90240)
    GT = p.scra(f"gt{tag}", [128, 32], BF16, 90368)
    BSB = p.scra(f"bsb{tag}", [128, 128], BF16, 96960)
    G1128 = p.scra(f"g1v{tag}", [128, 1024], BF16, 90688)
    TNB128 = p.scra(f"tnbv{tag}", [128, 2048], BF16, 92736)
    THT = p.scra(f"tht{tag}", [128, 64], BF16, 96832)
    BC4B128 = p.scra(f"bc4v{tag}", [128, 2048], BF16, 90112)
    XL4B128 = p.scra(f"xl4v{tag}", [128, 2048], BF16, 98304)
    PG4 = p.scra(f"pg4{tag}", [4, 2048], BF16, 32768)
    PB4 = p.scra(f"pb4{tag}", [4, 2048], BF16, 36864)
    U = p.scra(f"u{tag}", [128, TT * 2048], BF16, 0)
    OGp = p.scra(f"og{tag}", [128, 2048], F32, 32768)
    OBp = p.scra(f"ob{tag}", [128, 2048], F32, 40960)
    ybuf = p.scra(f"yb{tag}", [128, 2 * 512], F32, 49152)
    SGB = p.scra(f"sg{tag}", [128, 2 * 512], F32, 53248)
    gb_sb = p.scra(f"gb{tag}", [1, 2048], BF16, 57344)
    gbf = p.scra(f"gbf{tag}", [1, 2048], F32, 49152)
    t1 = p.scra(f"t1{tag}", [128, 2048], F32, 49152)
    wb = p.scra(f"wbf{tag}", [128, 2 * 512], BF16, 104448)

    g0 = gq.n
    @block.gpsimd
    def _(g):
        g.wait_ge(xlnt_done[0].h, xlnt_done[1])
        g.wait_ge(cbc[0].h, cbc[1])
        d = g.dma_start(out=ROWB[0:1, 0:2048], in_=X[127:128, 7 * C:8 * C])
        gq.n += 16
        d.then_inc(gq.h, 16)
        d = g.dma_start(out=S1[0:1, 22:23], in_=NB[127:128, 7:8])
        gq.n += 16
        d.then_inc(gq.h, 16)
        d = g.dma_start(out=S1[0:1, 23:24], in_=RS[127:128, 7:8])
        gq.n += 16
        d.then_inc(gq.h, 16)
        d = g.dma_start(out=IG[:], in_=p.t["in_g"].ap())
        gq.n += 16
        d.then_inc(gq.h, 16)
        d = g.dma_start(out=IB[:], in_=p.t["in_b"].ap())
        gq.n += 16
        d.then_inc(gq.h, 16)
        g.wait_ge(gq.h, gq.n)
    row_ld = gq.n
    s0 = ss.n
    @block.scalar
    def _(s):
        s.wait_ge(gq.h, row_ld)
        i = s.activation(ROWB[0:1, 2048:4096], ROWB[0:1, 0:2048],
                         AF.Identity, bias=S1[0:1, 22:23],
                         scale=S1[0:1, 23:24])
        ss.n += 1
        i.then_inc(ss.h, 1)
    v0 = vv.n
    @block.vector
    def _(v):
        v.wait_ge(gq.h, row_ld)
        im = v.memset(XL4B128[:], 0.0)
        p.vfence(v, im)
        v.wait_ge(ss.h, s0 + 1)
        i1 = v.reduce_sum(S1[0:1, 24:25], ROWB[0:1, 2048:4096], axis=AXX)
        p.vfence(v, i1)
        i2 = v.tensor_tensor(ROWB[0:1, 0:2048], ROWB[0:1, 2048:4096],
                             ROWB[0:1, 2048:4096], ALU.mult)
        p.vfence(v, i2)
        i3 = v.reduce_sum(S1[0:1, 25:26], ROWB[0:1, 0:2048], axis=AXX)
        p.vfence(v, i3)
        i4 = v.tensor_single_scalar(S1[0:1, 26:27], S1[0:1, 24:25], 1.0 / C,
                                    ALU.mult)
        p.vfence(v, i4)
        i5 = v.tensor_single_scalar(S1[0:1, 27:28], S1[0:1, 25:26], 1.0 / C,
                                    ALU.mult)
        p.vfence(v, i5)
        i6 = v.tensor_tensor(S1[0:1, 28:29], S1[0:1, 26:27], S1[0:1, 26:27],
                             ALU.mult)
        p.vfence(v, i6)
        i7 = v.tensor_tensor(S1[0:1, 27:28], S1[0:1, 27:28], S1[0:1, 28:29],
                             ALU.subtract)
        p.vfence(v, i7)
        iv = v.tensor_single_scalar(S1[0:1, 27:28], S1[0:1, 27:28], EPS_FFN,
                                    ALU.add)
        vv.n += 1
        iv.then_inc(vv.h, 1)
    rv0 = vv.n
    @block.scalar
    def _(s):
        s.wait_ge(vv.h, rv0)
        i = s.activation(S1[0:1, 29:30], S1[0:1, 27:28], AF.Sqrt)
        ss.n += 1
        i.then_inc(ss.h, 1)
    rs0 = ss.n
    @block.vector
    def _(v):
        v.wait_ge(ss.h, rs0)
        i0 = v.reciprocal(S1[0:1, 29:30], S1[0:1, 29:30])
        p.vfence(v, i0)
        i1 = v.tensor_tensor(S1[0:1, 30:31], S1[0:1, 26:27], S1[0:1, 29:30],
                             ALU.mult)
        p.vfence(v, i1)
        iv = v.tensor_single_scalar(S1[0:1, 30:31], S1[0:1, 30:31], -1.0,
                                    ALU.mult)
        vv.n += 1
        iv.then_inc(vv.h, 1)
    nb0 = vv.n
    @block.scalar
    def _(s):
        s.wait_ge(vv.h, nb0)
        i = s.activation(ROWB[0:1, 0:2048], ROWB[0:1, 2048:4096], AF.Identity,
                         bias=S1[0:1, 30:31], scale=S1[0:1, 29:30])
        ss.n += 1
        i.then_inc(ss.h, 1)
    lnr = ss.n
    @block.vector
    def _(v):
        v.wait_ge(ss.h, lnr)
        i1 = v.tensor_tensor(ROWB[0:1, 2048:4096], ROWB[0:1, 0:2048],
                             IG[0:1, :], ALU.mult)
        p.vfence(v, i1)
        iv = v.tensor_tensor(XL4B128[0:1, :], ROWB[0:1, 2048:4096], IB[0:1, :],
                             ALU.add)
        vv.n += 1
        iv.then_inc(vv.h, 1)
    xn_done = vv.n

    g1b, c1b = gq.n, cc.n
    @block.gpsimd
    def _(g):
        g.wait_ge(vv.h, xn_done)
        d = g.dma_start(out=p.t["th_in"].ap(), in_=XL4B128[0:1, :])
        gq.n += 16
        d.then_inc(gq.h, 16)
        g.wait_ge(gq.h, gq.n)
        icc = g.collective_compute(
            "AllGather", ALU.bypass, replica_groups=[list(range(8))],
            ins=[p.t["th_in"].ap().opt()], outs=[p.t["th_out"].ap().opt()])
        cc.n += 1
        icc.then_inc(cc.h, 1)
        g.wait_ge(cc.h, cc.n)
        for b in range(4):
            d = g.dma_start(out=XL4B128[b:b + 1, :],
                            in_=p.t["th_out"].ap()[2 * b + 1:2 * b + 2, :])
            gq.n += 16
            d.then_inc(gq.h, 16)
        d = g.dma_start(out=BB4[0:4, :],
                        in_=p.t["agg_b"].ap()[0:1, :].broadcast_to((4, C)))
        gq.n += 16
        d.then_inc(gq.h, 16)
        d = g.dma_start(out=S1[0:4, 21:22], in_=p.t["bsel"].ap()[0:4, 0:1])
        gq.n += 16
        d.then_inc(gq.h, 16)
        if DEBUG_DUMP == "l0ffn":
            d = g.dma_start(out=p.t["dbg"].ap()[8:12, 0:1024],
                            in_=XL4B128[0:4, :].bitcast(F32))
            gq.n += 16
            d.then_inc(gq.h, 16)
        g.wait_ge(gq.h, gq.n)
    xl4_ld = gq.n
    vbs = vv.n
    @block.vector
    def _(v):
        v.wait_ge(gq.h, xl4_ld)
        i0 = v.memset(BSB[:], 0.0)
        p.vfence(v, i0)
        i1 = v.memset(BSB[0:4, :], 1.0)
        p.vfence(v, i1)
        iv = v.tensor_scalar_mul(BSB[0:4, :], BSB[0:4, :], S1[0:4, 21:22])
        vv.n += 1
        iv.then_inc(vv.h, 1)
    bsb_done = vv.n

    tv0 = tv.n
    with nc.psum_tensor(f"pxt{tag}", [128, 16 * 128], BF16) as PXT:
        @block.tensor
        def _(t):
            t.wait_ge(gq.h, xl4_ld)
            for kc in range(CT):
                it = t.transpose(PXT[:, kc * 128:(kc + 1) * 128],
                                 XL4B128[:, kc * 128:(kc + 1) * 128],
                                 IDN[:])
            tv.n += 1
            it.then_inc(tv.h, 1)
        s1b = ss.n
        @block.scalar
        def _(s):
            s.wait_ge(tv.h, tv0 + 1)
            i = s.copy(XLT[:],
                       PXT[:].rearrange("p (k f) -> p k f", f=128)[:, :, 0:4])
            ss.n += 1
            i.then_inc(ss.h, 1)
    xlt_done = ss.n

    def emit_stream4(stag, wdram, lhsT_sb, bias4, dst_write, w_ready, t_ready,
                     sync_extra=None):
        sq0, tv0b, ev0 = sq.n, tv.n, ev.n
        with (nc.psum_tensor(f"pa{stag}a", [4, 512], F32) as pa,
              nc.psum_tensor(f"pa{stag}b", [4, 512], F32) as pb):
            pas = [pa, pb]
            @block.sync
            def _(sy):
                sy.wait_ge(w_ready[0].h, w_ready[1])
                if sync_extra is not None:
                    sy.wait_ge(sync_extra[0].h, sync_extra[1])
                for nch in range(4):
                    for kc in range(CT):
                        idx = nch * CT + kc
                        if idx >= 2:
                            sy.wait_ge(tv.h, tv0b + idx - 1)
                        d = sy.dma_start(
                            out=wb[:, (idx % 2) * 512:(idx % 2 + 1) * 512],
                            in_=wdram[kc * 128:(kc + 1) * 128,
                                      nch * 512:(nch + 1) * 512])
                        sq.n += 16
                        d.then_inc(sq.h, 16)
            @block.tensor
            def _(t):
                t.wait_ge(t_ready[0].h, t_ready[1])
                for nch in range(4):
                    for kc in range(CT):
                        idx = nch * CT + kc
                        t.wait_ge(sq.h, sq0 + 16 * min(idx + 2, 4 * CT))
                        if kc == 0 and nch >= 2:
                            t.wait_ge(ev.h, ev0 + nch - 1)
                        it = t.matmul(pas[nch % 2][:],
                                      lhsT_sb[:, kc * 4:(kc + 1) * 4],
                                      wb[:, (idx % 2) * 512:(idx % 2 + 1) * 512],
                                      start=(kc == 0), stop=(kc == CT - 1))
                        tv.n += 1
                        it.then_inc(tv.h, 1)
            @block.vector
            def _(v):
                for nch in range(4):
                    v.wait_ge(tv.h, tv0b + (nch + 1) * CT)
                    iv = dst_write(v, nch, pas[nch % 2], bias4)
                    ev.n += 1
                    iv.then_inc(ev.h, 1)
        return (ev, ev.n)

    def th_write(v, nch, bank, bias4):
        return v.tensor_tensor(TH[0:4, nch * 512:(nch + 1) * 512], bank[:],
                               bias4[0:4, nch * 512:(nch + 1) * 512], ALU.add)

    agg_done = emit_stream4(f"ag{tag}", p.t["agg_bf"].ap(), XLT, BB4, th_write,
                            cag, (ss, xlt_done), sync_extra=(gq, xl4_ld))
    agg_tv = tv.n
    if DEBUG_DUMP == "l0ffn":
        @block.gpsimd
        def _(g):
            g.wait_ge(agg_done[0].h, agg_done[1])
            d = g.dma_start(out=p.t["dbg"].ap()[0:4, 0:1024], in_=TH[0:4, 0:1024])
            gq.n += 16
            d.then_inc(gq.h, 16)
            d = g.dma_start(out=p.t["dbg"].ap()[28:32, 0:1024],
                            in_=TH[0:4, 1024:2048])
            gq.n += 16
            d.then_inc(gq.h, 16)
            d = g.dma_start(out=p.t["dbg"].ap()[56:60, 0:1024],
                            in_=BB4[0:4, 0:1024])
            gq.n += 16
            d.then_inc(gq.h, 16)
            d = g.dma_start(out=p.t["dbg"].ap()[60:64, 0:1024],
                            in_=BB4[0:4, 1024:2048])
            gq.n += 16
            d.then_inc(gq.h, 16)

    g2b = gq.n
    @block.gpsimd
    def _(g):
        g.wait_ge(vv.h, xn_done)
        d = g.dma_start(out=TN[0:4, :],
                        in_=p.t["pn_g"].ap()[0:1, :].broadcast_to((4, C)))
        gq.n += 16
        d.then_inc(gq.h, 16)
        g.wait_ge(gq.h, gq.n)
    pg_st = gq.n
    v2b = vv.n
    @block.vector
    def _(v):
        v.wait_ge(gq.h, pg_st)
        iv = v.tensor_copy(PG4[0:4, :], TN[0:4, :])
        vv.n += 1
        iv.then_inc(vv.h, 1)
    pg_done = vv.n
    g3b = gq.n
    @block.gpsimd
    def _(g):
        g.wait_ge(vv.h, pg_done)
        d = g.dma_start(out=TN[0:4, :],
                        in_=p.t["pn_b"].ap()[0:1, :].broadcast_to((4, C)))
        gq.n += 16
        d.then_inc(gq.h, 16)
        g.wait_ge(gq.h, gq.n)
    pb_st = gq.n
    @block.vector
    def _(v):
        v.wait_ge(gq.h, pb_st)
        iv = v.tensor_copy(PB4[0:4, :], TN[0:4, :])
        vv.n += 1
        iv.then_inc(vv.h, 1)
    pb_done = vv.n
    @block.vector
    def _(v):
        i1 = v.memset(TNB128[:], 0.0)
        p.vfence(v, i1)
        iv = v.memset(G1128[:], 0.0)
        vv.n += 1
        iv.then_inc(vv.h, 1)
    zj_done = vv.n

    th_ready = (ev, agg_done[1])
    d12_ld = pg_st
    with (nc.psum_tensor(f"ptt{tag}", [128, 16 * 128], BF16) as PTT,
          nc.psum_tensor(f"po1{tag}a", [4, 512], F32) as poa,
          nc.psum_tensor(f"po1{tag}b", [4, 512], F32) as pob,
          contextlib.ExitStack() as pes):
        pos = [poa, pob]
        p2s = [pes.enter_context(nc.psum_tensor(f"p2{tag}{i}", [4, 512], F32))
               for i in range(4)]
        for k in range(4):
            v3 = vv.n
            @block.vector
            def _(v):
                if k == 0:
                    v.wait_ge(th_ready[0].h, th_ready[1])
                    v.wait_ge(vv.h, pb_done)
                i1 = v.reduce_sum(S1[0:4, 24:25], TH[0:4, :], axis=AXX)
                p.vfence(v, i1)
                i2 = v.tensor_tensor(TN[0:4, :], TH[0:4, :], TH[0:4, :],
                                     ALU.mult)
                p.vfence(v, i2)
                i3 = v.reduce_sum(S1[0:4, 25:26], TN[0:4, :], axis=AXX)
                p.vfence(v, i3)
                i4 = v.tensor_single_scalar(S1[0:4, 26:27], S1[0:4, 24:25],
                                            1.0 / C, ALU.mult)
                p.vfence(v, i4)
                i5 = v.tensor_single_scalar(S1[0:4, 27:28], S1[0:4, 25:26],
                                            1.0 / C, ALU.mult)
                p.vfence(v, i5)
                i6 = v.tensor_tensor(S1[0:4, 28:29], S1[0:4, 26:27],
                                     S1[0:4, 26:27], ALU.mult)
                p.vfence(v, i6)
                i7 = v.tensor_tensor(S1[0:4, 27:28], S1[0:4, 27:28],
                                     S1[0:4, 28:29], ALU.subtract)
                p.vfence(v, i7)
                iv = v.tensor_single_scalar(S1[0:4, 27:28], S1[0:4, 27:28],
                                            EPS_FFN, ALU.add)
                vv.n += 1
                iv.then_inc(vv.h, 1)
            lv1 = vv.n
            s3 = ss.n
            @block.scalar
            def _(s):
                s.wait_ge(vv.h, lv1)
                i1 = s.activation(S1[0:4, 29:30], S1[0:4, 27:28], AF.Sqrt)
                ss.n += 1
                i1.then_inc(ss.h, 1)
            ls1 = ss.n
            @block.vector
            def _(v):
                v.wait_ge(ss.h, ls1)
                i0 = v.reciprocal(S1[0:4, 29:30], S1[0:4, 29:30])
                p.vfence(v, i0)
                i1 = v.tensor_tensor(S1[0:4, 30:31], S1[0:4, 26:27],
                                     S1[0:4, 29:30], ALU.mult)
                p.vfence(v, i1)
                iv = v.tensor_single_scalar(S1[0:4, 30:31], S1[0:4, 30:31],
                                            -1.0, ALU.mult)
                vv.n += 1
                iv.then_inc(vv.h, 1)
            lv2 = vv.n
            @block.scalar
            def _(s):
                s.wait_ge(vv.h, lv2)
                i = s.activation(TN[0:4, :], TH[0:4, :], AF.Identity,
                                 bias=S1[0:4, 30:31], scale=S1[0:4, 29:30])
                ss.n += 1
                i.then_inc(ss.h, 1)
            ls2 = ss.n
            @block.vector
            def _(v):
                v.wait_ge(ss.h, ls2)
                i1 = v.tensor_tensor(TN[0:4, :], TN[0:4, :], PG4[0:4, :],
                                     ALU.mult)
                p.vfence(v, i1)
                iv = v.tensor_tensor(TNB128[0:4, :], TN[0:4, :], PB4[0:4, :],
                                     ALU.add)
                vv.n += 1
                iv.then_inc(vv.h, 1)
            tnb_done = vv.n

            g4 = gq.n
            @block.gpsimd
            def _(g):
                g.wait_ge(vv.h, tnb_done)
                d = g.dma_start(
                    out=BB4[0:4, 0:1024],
                    in_=p.t["d1bs"].ap()[0:1, :].broadcast_to((4, 1024)))
                gq.n += 16
                d.then_inc(gq.h, 16)
                g.wait_ge(gq.h, gq.n)
            d1b_ld = gq.n

            t4 = tv.n
            @block.tensor
            def _(t):
                t.wait_ge(vv.h, tnb_done)
                if k == 0:
                    t.wait_ge(gq.h, d12_ld)
                    t.wait_ge(vv.h, zj_done)
                for kc in range(CT):
                    it = t.transpose(PTT[:, kc * 128:(kc + 1) * 128],
                                     TNB128[:, kc * 128:(kc + 1) * 128],
                                     IDN[:])
                tv.n += 1
                it.then_inc(tv.h, 1)
            s4 = ss.n
            @block.scalar
            def _(s):
                s.wait_ge(tv.h, t4 + 1)
                i = s.copy(TNT[:],
                           PTT[:].rearrange("p (k f) -> p k f", f=128)[:, :, 0:4])
                ss.n += 1
                i.then_inc(ss.h, 1)
            tnt_done = ss.n

            sq8, t5 = sq.n, tv.n
            @block.sync
            def _(sy):
                if k == 0:
                    sy.wait_ge(cd1[0].h, cd1[1])
                    sy.wait_ge(tv.h, agg_tv)
                for c2 in range(2):
                    for kc in range(CT):
                        idx = c2 * CT + kc
                        sy.wait_ge(tv.h, t5 + idx - 1)
                        d = sy.dma_start(
                            out=wb[:, (idx % 2) * 512:(idx % 2 + 1) * 512],
                            in_=p.t["d1s_bf"].ap()[kc * 128:(kc + 1) * 128,
                                                   c2 * 512:(c2 + 1) * 512])
                        sq.n += 16
                        d.then_inc(sq.h, 16)
            @block.tensor
            def _(t):
                t.wait_ge(ss.h, tnt_done)
                for c2 in range(2):
                    for kc in range(CT):
                        idx = c2 * CT + kc
                        t.wait_ge(sq.h, sq8 + 16 * min(idx + 2, 2 * CT))
                        it = t.matmul(
                            pos[c2][:],
                            TNT[:, kc * 4:(kc + 1) * 4],
                            wb[:, (idx % 2) * 512:(idx % 2 + 1) * 512],
                            start=(kc == 0), stop=(kc == CT - 1))
                        tv.n += 1
                        it.then_inc(tv.h, 1)
            v5 = vv.n
            @block.vector
            def _(v):
                v.wait_ge(gq.h, d1b_ld)
                for c2 in range(2):
                    v.wait_ge(tv.h, t5 + (c2 + 1) * CT)
                    iv = v.tensor_tensor(TN[0:4, c2 * 512:(c2 + 1) * 512],
                                         pos[c2][:],
                                         BB4[0:4, c2 * 512:(c2 + 1) * 512],
                                         ALU.add)
                    vv.n += 1
                    iv.then_inc(vv.h, 1)
            s5 = ss.n
            @block.scalar
            def _(s):
                for c2 in range(2):
                    s.wait_ge(vv.h, v5 + c2 + 1)
                    i = s.activation(G1128[0:4, c2 * 512:(c2 + 1) * 512],
                                     TN[0:4, c2 * 512:(c2 + 1) * 512], AF.Gelu)
                    ss.n += 1
                    i.then_inc(ss.h, 1)
            g1_done = ss.n

            g5 = gq.n
            @block.gpsimd
            def _(g):
                g.wait_ge(vv.h, v5 + 2)
                d = g.dma_start(
                    out=BB4[0:4, :],
                    in_=p.t["d2_b"].ap()[0:1, :].broadcast_to((4, C)))
                gq.n += 16
                d.then_inc(gq.h, 16)
                g.wait_ge(gq.h, gq.n)
            d2b_ld = gq.n

            t6 = tv.n
            @block.tensor
            def _(t):
                t.wait_ge(ss.h, g1_done)
                for kt in range(8):
                    it = t.transpose(PTT[:, kt * 128:(kt + 1) * 128],
                                     G1128[:, kt * 128:(kt + 1) * 128],
                                     IDN[:])
                tv.n += 1
                it.then_inc(tv.h, 1)
            s6 = ss.n
            @block.scalar
            def _(s):
                s.wait_ge(tv.h, t6 + 1)
                i = s.copy(GT[:],
                           PTT[:, 0:1024].rearrange("p (k f) -> p k f",
                                                    f=128)[:, :, 0:4])
                ss.n += 1
                i.then_inc(ss.h, 1)
            gt_done = ss.n

            sq9, t7 = sq.n, tv.n
            @block.sync
            def _(sy):
                if k == 0:
                    sy.wait_ge(cd2[0].h, cd2[1])
                for nch in range(4):
                    for kt in range(8):
                        idx = nch * 8 + kt
                        sy.wait_ge(tv.h, t7 + idx - 1)
                        d = sy.dma_start(
                            out=wb[:, (idx % 2) * 512:(idx % 2 + 1) * 512],
                            in_=p.t["d2s_bf"].ap()[kt * 128:(kt + 1) * 128,
                                                   nch * 512:(nch + 1) * 512])
                        sq.n += 16
                        d.then_inc(sq.h, 16)
            @block.tensor
            def _(t):
                t.wait_ge(ss.h, gt_done)
                for nch in range(4):
                    for kt in range(8):
                        idx = nch * 8 + kt
                        t.wait_ge(sq.h, sq9 + 16 * min(idx + 2, 32))
                        it = t.matmul(
                            p2s[nch][:],
                            GT[:, kt * 4:(kt + 1) * 4],
                            wb[:, (idx % 2) * 512:(idx % 2 + 1) * 512],
                            start=(kt == 0), stop=(kt == 7))
                        tv.n += 1
                        it.then_inc(tv.h, 1)
            d2mm = tv.n

            s7 = ss.n
            @block.scalar
            def _(s):
                s.wait_ge(tv.h, d2mm)
                for nch in range(4):
                    i = s.copy(TN[0:4, nch * 512:(nch + 1) * 512],
                               p2s[nch][:])
                    ss.n += 1
                    i.then_inc(ss.h, 1)
            p2ev = ss.n
            g6, c6 = gq.n, cc.n
            @block.gpsimd
            def _(g):
                g.wait_ge(ss.h, p2ev)
                d = g.dma_start(out=p.t["ar_in"].ap(), in_=TN[0:4, :])
                gq.n += 16
                d.then_inc(gq.h, 16)
                g.wait_ge(gq.h, gq.n)
                icc = g.collective_compute(
                    "AllReduce", ALU.add, replica_groups=[list(range(8))],
                    ins=[p.t["ar_in"].ap().opt()],
                    outs=[p.t["ar_out"].ap().opt()])
                cc.n += 1
                icc.then_inc(cc.h, 1)
                g.wait_ge(cc.h, cc.n)
                d = g.dma_start(out=TN[0:4, :], in_=p.t["ar_out"].ap())
                gq.n += 16
                d.then_inc(gq.h, 16)
                g.wait_ge(gq.h, gq.n)
            ar_ld = gq.n

            v7 = vv.n
            @block.vector
            def _(v):
                v.wait_ge(gq.h, ar_ld)
                v.wait_ge(gq.h, d2b_ld)
                i1 = v.tensor_tensor(TH[0:4, :], TH[0:4, :], TN[0:4, :],
                                     ALU.add)
                p.vfence(v, i1)
                iv = v.tensor_tensor(TH[0:4, :], TH[0:4, :], BB4[0:4, :],
                                     ALU.add)
                vv.n += 1
                iv.then_inc(vv.h, 1)
            th_ready = (vv, vv.n)
            if DEBUG_DUMP == "l0ffn":
                kk = k
                @block.gpsimd
                def _(g):
                    g.wait_ge(vv.h, th_ready[1])
                    d = g.dma_start(out=p.t["dbg"].ap()[32 + 4 * kk:36 + 4 * kk, 0:1024],
                                    in_=TH[0:4, 0:1024])
                    gq.n += 16
                    d.then_inc(gq.h, 16)
                    if kk == 0:
                        d = g.dma_start(out=p.t["dbg"].ap()[12:16, 0:1024],
                                        in_=TNB128[0:4, :].bitcast(F32))
                        gq.n += 16
                        d.then_inc(gq.h, 16)
                        d = g.dma_start(out=p.t["dbg"].ap()[:, 1024:1056],
                                        in_=TNT[:].bitcast(F32))
                        gq.n += 16
                        d.then_inc(gq.h, 16)
                        d = g.dma_start(out=p.t["dbg"].ap()[16:20, 0:512],
                                        in_=G1128[0:4, :].bitcast(F32))
                        gq.n += 16
                        d.then_inc(gq.h, 16)
                        d = g.dma_start(out=p.t["dbg"].ap()[24:28, 0:1024],
                                        in_=TN[0:4, 0:1024])
                        gq.n += 16
                        d.then_inc(gq.h, 16)

    if DEBUG_DUMP == "l0ffn":
        @block.gpsimd
        def _(g):
            g.wait_ge(th_ready[0].h, th_ready[1])
            d = g.dma_start(out=p.t["dbg"].ap()[4:8, 0:1024], in_=TH[0:4, 0:1024])
            gq.n += 16
            d.then_inc(gq.h, 16)

    v8 = vv.n
    @block.vector
    def _(v):
        v.wait_ge(th_ready[0].h, th_ready[1])
        iv = v.tensor_copy(TNB128[0:4, :], TH[0:4, :])
        vv.n += 1
        iv.then_inc(vv.h, 1)
    thb_done = vv.n
    t8 = tv.n
    with nc.psum_tensor(f"ptht{tag}", [128, 16 * 128], BF16) as PTH:
        @block.tensor
        def _(t):
            t.wait_ge(vv.h, thb_done)
            for kc in range(CT):
                it = t.transpose(PTH[:, kc * 128:(kc + 1) * 128],
                                 TNB128[:, kc * 128:(kc + 1) * 128],
                                 IDN[:])
            tv.n += 1
            it.then_inc(tv.h, 1)
        s8 = ss.n
        @block.scalar
        def _(s):
            s.wait_ge(tv.h, t8 + 1)
            i = s.copy(THT[:],
                       PTH[:].rearrange("p (k f) -> p k f", f=128)[:, :, 0:4])
            ss.n += 1
            i.then_inc(ss.h, 1)
    tht_done = ss.n
    vz2 = vv.n
    @block.vector
    def _(v):
        v.wait_ge(tv.h, t8 + 1)
        iv = v.memset(BC4B128[:], 0.0)
        vv.n += 1
        iv.then_inc(vv.h, 1)
    bcz_done = vv.n
    g8 = gq.n
    @block.gpsimd
    def _(g):
        g.wait_ge(th_ready[0].h, th_ready[1])
        d = g.dma_start(out=BB4[0:4, :],
                        in_=p.t["bc_b"].ap()[0:1, :].broadcast_to((4, C)))
        gq.n += 16
        d.then_inc(gq.h, 16)
        g.wait_ge(gq.h, gq.n)
    bcb_ld = gq.n

    def bc_write(v, nch, bank, bias4):
        return v.tensor_tensor(BC4B128[0:4, nch * 512:(nch + 1) * 512], bank[:],
                               bias4[0:4, nch * 512:(nch + 1) * 512], ALU.add)

    bc_done = emit_stream4(f"bc{tag}", p.t["bc_bf"].ap(), THT, BB4, bc_write,
                           cbc, (ss, tht_done), sync_extra=(gq, bcb_ld))
    t9 = tv.n
    s9 = ss.n
    with (nc.psum_tensor(f"pse{tag}a", [128, 512], F32) as psa,
          nc.psum_tensor(f"pse{tag}b", [128, 512], F32) as psb):
        pss = [psa, psb]
        @block.tensor
        def _(t):
            t.wait_ge(bc_done[0].h, bc_done[1])
            t.wait_ge(vv.h, bsb_done)
            for nch in range(4):
                if nch >= 2:
                    t.wait_ge(ss.h, s9 + nch - 1)
                it = t.matmul(pss[nch % 2][:], BSB[:],
                              BC4B128[:, nch * 512:(nch + 1) * 512],
                              start=True, stop=True)
                tv.n += 1
                it.then_inc(tv.h, 1)
        @block.scalar
        def _(s):
            for nch in range(4):
                s.wait_ge(tv.h, t9 + nch + 1)
                i = s.copy(BCB[:, nch * 512:(nch + 1) * 512],
                           pss[nch % 2][:])
                ss.n += 1
                i.then_inc(ss.h, 1)
    bcb_done = ss.n
    bc_tv = tv.n
    if DEBUG_DUMP == "l0ffn":
        @block.gpsimd
        def _(g):
            g.wait_ge(ss.h, bcb_done)
            d = g.dma_start(out=p.t["dbg"].ap()[20:24, 0:1024], in_=BCB[0:4, 0:1024])
            gq.n += 16
            d.then_inc(gq.h, 16)

    g9 = gq.n
    @block.gpsimd
    def _(g):
        g.wait_ge(tv.h, d2mm)
        d = g.dma_start(out=gbf[:], in_=p.t["gate_b"].ap())
        gq.n += 16
        d.then_inc(gq.h, 16)
        g.wait_ge(gq.h, gq.n)
    gb_st = gq.n
    v9 = vv.n
    @block.vector
    def _(v):
        v.wait_ge(gq.h, gb_st)
        iv = v.tensor_copy(gb_sb[:], gbf[:])
        vv.n += 1
        iv.then_inc(vv.h, 1)
    gb_done = vv.n
    g10 = gq.n
    @block.gpsimd
    def _(g):
        g.wait_ge(tv.h, d2mm)
        d = g.dma_start(out=OGp[:],
                        in_=p.t["out_g"].ap()[0:1, :].broadcast_to((128, C)))
        gq.n += 16
        d.then_inc(gq.h, 16)
        d = g.dma_start(out=OBp[:],
                        in_=p.t["out_b"].ap()[0:1, :].broadcast_to((128, C)))
        gq.n += 16
        d.then_inc(gq.h, 16)
        g.wait_ge(gq.h, gq.n)
    ogb_ld = gq.n
    v10 = vv.n
    @block.vector
    def _(v):
        v.wait_ge(gq.h, ogb_ld)
        i1 = v.tensor_single_scalar(OGp[:], OGp[:], 0.5, ALU.mult)
        p.vfence(v, i1)
        iv = v.tensor_single_scalar(OBp[:], OBp[:], 0.5, ALU.mult)
        vv.n += 1
        iv.then_inc(vv.h, 1)
    ogb_done = vv.n

    sqg, tvg, evg = sq.n, tv.n, ev.n
    ssg = ss.n
    with contextlib.ExitStack() as pes:
        banks = [pes.enter_context(
            nc.psum_tensor(f"gg{tag}{i}", [128, 512], F32)) for i in range(8)]
        @block.sync
        def _(sy):
            sy.wait_ge(cga[0].h, cga[1])
            sy.wait_ge(tv.h, bc_tv)
            for nch in range(4):
                for kc in range(CT):
                    idx = nch * CT + kc
                    if idx >= 2:
                        pn, pk = divmod(idx - 2, CT)
                        sy.wait_ge(tv.h, tvg + pn * (CT + 1) + pk + 1)
                    d = sy.dma_start(
                        out=wb[:, (idx % 2) * 512:(idx % 2 + 1) * 512],
                        in_=p.t["gate_bf"].ap()[kc * 128:(kc + 1) * 128,
                                                nch * 512:(nch + 1) * 512])
                    sq.n += 16
                    d.then_inc(sq.h, 16)
        @block.tensor
        def _(t):
            t.wait_ge(vv.h, gb_done)
            t.wait_ge(ss.h, bcb_done)
            for nch in range(4):
                for kc in range(CT):
                    idx = nch * CT + kc
                    t.wait_ge(sq.h, sqg + 16 * min(idx + 2, 4 * CT))
                    if kc == 0 and nch > 0:
                        t.wait_ge(ev.h, evg + nch * TT)
                    wap = wb[:, (idx % 2) * 512:(idx % 2 + 1) * 512]
                    for m in range(TT):
                        it = t.matmul(
                            banks[m][:],
                            XLNT[:, kc * TL + m * 128: kc * TL + (m + 1) * 128],
                            wap, start=(kc == 0), stop=False)
                    tv.n += 1
                    it.then_inc(tv.h, 1)
                for m in range(TT):
                    it = t.matmul(banks[m][:], ONESR[:],
                                  gb_sb[0:1, nch * 512:(nch + 1) * 512],
                                  start=False, stop=True)
                tv.n += 1
                it.then_inc(tv.h, 1)
        @block.scalar
        def _(s):
            for nch in range(4):
                s.wait_ge(tv.h, tvg + (nch + 1) * (CT + 1))
                for m in range(TT):
                    sl = (nch * TT + m) % 2
                    if nch * TT + m >= 2:
                        s.wait_ge(ev.h, evg + nch * TT + m - 1)
                    i1 = s.activation(SGB[:, sl * 512:(sl + 1) * 512],
                                      banks[m][:], AF.Sigmoid)
                    ss.n += 1
                    i1.then_inc(ss.h, 1)
                    i2 = s.activation(ybuf[:, sl * 512:(sl + 1) * 512],
                                      X[:, m * C + nch * 512:
                                        m * C + (nch + 1) * 512],
                                      AF.Identity, bias=NB[:, m:m + 1],
                                      scale=RS[:, m:m + 1])
                    ss.n += 1
                    i2.then_inc(ss.h, 1)
        @block.vector
        def _(v):
            for nch in range(4):
                for m in range(TT):
                    sl = (nch * TT + m) % 2
                    v.wait_ge(ss.h, ssg + 2 * (nch * TT + m + 1))
                    i1 = v.tensor_tensor(SGB[:, sl * 512:(sl + 1) * 512],
                                         SGB[:, sl * 512:(sl + 1) * 512],
                                         BCB[:, nch * 512:(nch + 1) * 512],
                                         ALU.mult)
                    p.vfence(v, i1)
                    iv = v.tensor_tensor(
                        U[:, m * 2048 + nch * 512: m * 2048 + (nch + 1) * 512],
                        ybuf[:, sl * 512:(sl + 1) * 512],
                        SGB[:, sl * 512:(sl + 1) * 512], ALU.add)
                    ev.n += 1
                    iv.then_inc(ev.h, 1)
    u_done = ev.n

    vb, sb_ = vv.n, ss.n
    @block.scalar
    def _(s):
        for m in range(TT):
            s.wait_ge(vv.h, vb + 4 * m)
            s.wait_ge(ev.h, u_done)
            i = s.square(t1[:], U[:, m * 2048:(m + 1) * 2048])
            ss.n += 1
            i.then_inc(ss.h, 1)
            s.wait_ge(vv.h, vb + 4 * m + 2)
            i2 = s.activation(RS[:, 24 + m:25 + m], RS[:, 16 + m:17 + m],
                              AF.Sqrt)
            ss.n += 1
            i2.then_inc(ss.h, 1)
            s.wait_ge(vv.h, vb + 4 * m + 3)
            i3 = s.activation(t1[:], U[:, m * 2048:(m + 1) * 2048],
                              AF.Identity, bias=NB[:, 16 + m:17 + m],
                              scale=RS[:, 24 + m:25 + m])
            ss.n += 1
            i3.then_inc(ss.h, 1)
    @block.vector
    def _(v):
        v.wait_ge(ev.h, u_done)
        v.wait_ge(vv.h, ogb_done)
        for m in range(TT):
            i1 = v.reduce_sum(MU[:, 16 + m:17 + m],
                              U[:, m * 2048:(m + 1) * 2048], axis=AXX)
            vv.n += 1
            i1.then_inc(vv.h, 1)
            v.wait_ge(ss.h, sb_ + 3 * m + 1)
            i2 = v.reduce_sum(RS[:, 16 + m:17 + m], t1[:], axis=AXX)
            p.vfence(v, i2)
            i3 = v.tensor_single_scalar(RS[:, 16 + m:17 + m],
                                        RS[:, 16 + m:17 + m], 1.0 / C,
                                        ALU.mult)
            p.vfence(v, i3)
            i4 = v.tensor_single_scalar(MU[:, 16 + m:17 + m],
                                        MU[:, 16 + m:17 + m], 1.0 / C,
                                        ALU.mult)
            p.vfence(v, i4)
            i5 = v.tensor_tensor(NB[:, 16 + m:17 + m], MU[:, 16 + m:17 + m],
                                 MU[:, 16 + m:17 + m], ALU.mult)
            p.vfence(v, i5)
            i6 = v.tensor_tensor(RS[:, 16 + m:17 + m], RS[:, 16 + m:17 + m],
                                 NB[:, 16 + m:17 + m], ALU.subtract)
            p.vfence(v, i6)
            i7 = v.tensor_single_scalar(RS[:, 16 + m:17 + m],
                                        RS[:, 16 + m:17 + m], EPS_FFN,
                                        ALU.add)
            vv.n += 1
            i7.then_inc(vv.h, 1)
            v.wait_ge(ss.h, sb_ + 3 * m + 2)
            i8r = v.reciprocal(RS[:, 24 + m:25 + m], RS[:, 24 + m:25 + m])
            p.vfence(v, i8r)
            i8 = v.tensor_tensor(NB[:, 16 + m:17 + m], MU[:, 16 + m:17 + m],
                                 RS[:, 24 + m:25 + m], ALU.mult)
            p.vfence(v, i8)
            i9 = v.tensor_single_scalar(NB[:, 16 + m:17 + m],
                                        NB[:, 16 + m:17 + m], -1.0, ALU.mult)
            vv.n += 1
            i9.then_inc(vv.h, 1)
            v.wait_ge(ss.h, sb_ + 3 * m + 3)
            iA = v.tensor_tensor(t1[:], t1[:], OGp[:], ALU.mult)
            p.vfence(v, iA)
            iB = v.tensor_tensor(t1[:], t1[:], OBp[:], ALU.add)
            p.vfence(v, iB)
            iC = v.tensor_tensor(X[:, m * C:(m + 1) * C], X[:, m * C:(m + 1) * C],
                                 t1[:], ALU.add)
            vv.n += 1
            iC.then_inc(vv.h, 1)
    return (vv, vv.n)


def emit_l0attn(p, tag, x_ready):
    nc, block = p.nc, p.block
    sq, gq, cc = p.sem("sq"), p.sem("gq"), p.sem("cc")
    vv, ss, tv, ev = p.sem("vv"), p.sem("ss"), p.sem("tv"), p.sem("ev")
    X, XLNT, IDN, ONESR = p.s["X"], p.s["XLNT"], p.s["IDN"], p.s["ONESR"]
    QRS = p.s["QRS"]

    cwq = emit_cast(p, "Wq", "Wq_bf", C, 2048)
    cwk = emit_cast(p, "Wk", "Wk_bf", C, 512)
    cwv = emit_cast(p, "Wv", "Wv_bf", C, 512)
    cwo = emit_cast(p, "Wo", "Wo_bf", H * DH, 2048)
    xlnt_done = emit_ln_to_xlnt(p, x_ready, tag)

    q_tm = p.scra(f"qtm{tag}", [128, TT * 2048], BF16, 0)
    k_tm = p.scra(f"ktm{tag}", [128, TT * 512], BF16, 32768)
    v_tm = p.scra(f"vtm{tag}", [128, TT * 512], BF16, 40960)
    knTm = p.scra(f"knm{tag}", [128, HKV * TL], BF16, 49152)
    qnT = p.scra(f"qnT{tag}", [128, 16 * TL], BF16, 32768)
    knT = p.scra(f"knT{tag}", [128, HKV * 2048], BF16, 65536)
    v_full = p.scra(f"vf{tag}", [128, 16 * 512], BF16, 81920)
    pT = p.scra(f"pT{tag}", [128, 16 * TL], BF16, 0)
    rb_sb = p.scra(f"rb{tag}", [128, TL], BF16, 102400)
    sc2 = p.at(f"sc2{tag}", [128, 2048], F32, OFF_XLNT)
    nr_t = p.at(f"nrt{tag}", [128, 2 * 2048], BF16, OFF_XLNT + 8192)
    aoT = p.at(f"aoT{tag}", [128, 16 * TL], BF16, OFF_XLNT)

    def lhsT_fn(kc, m):
        return XLNT[:, kc * TL + m * 128: kc * TL + (m + 1) * 128]

    def mk_evac(dst, ncol):
        def evac(nch, m, bank, mm_done, evs):
            @block.scalar
            def _(sc):
                sc.wait_ge(mm_done[0].h, mm_done[1])
                ic = sc.copy(dst[:, m * ncol + nch * 512:
                                 m * ncol + (nch + 1) * 512], bank[:])
                evs.n += 1
                ic.then_inc(evs.h, 1)
        return evac

    tv = p.sem("tv")
    r1 = emit_wstream_mm(p, f"q{tag}", p.t["Wq_bf"].ap(), 2048, lhsT_fn,
                         mk_evac(q_tm, 2048), cwq, t_ready=xlnt_done)
    r2 = emit_wstream_mm(p, f"k{tag}", p.t["Wk_bf"].ap(), 512, lhsT_fn,
                         mk_evac(k_tm, 512), cwk, ready2=(tv, tv.n), t_ready=r1)
    r3 = emit_wstream_mm(p, f"v{tag}", p.t["Wv_bf"].ap(), 512, lhsT_fn,
                         mk_evac(v_tm, 512), cwv, ready2=(tv, tv.n), t_ready=r2)
    tv_after_v = tv.n

    def emit_rms(src, nh, dst_write, done_after):
        base = 0 if nh == 16 else 192
        ncol = nh * 128
        s0, v0, t0 = ss.n, vv.n, tv.n
        @block.scalar
        def _(sc):
            sc.wait_ge(done_after[0].h, done_after[1])
            for m in range(TT):
                if m >= 1:
                    sc.wait_ge(vv.h, v0 + m)
                i2 = sc.square(sc2[:, 0:ncol], src[:, m * ncol:(m + 1) * ncol])
                ss.n += 1
                i2.then_inc(ss.h, 1)
        @block.vector
        def _(v):
            for m in range(TT):
                v.wait_ge(ss.h, s0 + m + 1)
                a3 = sc2[:, 0:ncol].rearrange("p (h f) -> p h f", f=128)
                iv = v.tensor_reduce(QRS[:, base + m * nh:base + (m + 1) * nh],
                                     a3, axis=AXX, op=ALU.add)
                vv.n += 1
                iv.then_inc(vv.h, 1)
            v.wait_ge(vv.h, vv.n)
            i1 = v.tensor_scalar(QRS[:, base:base + TT * nh],
                                 QRS[:, base:base + TT * nh],
                                 1.0 / 128.0, EPS_RMS, ALU.mult, ALU.add)
            vv.n += 1
            i1.then_inc(vv.h, 1)
        pre = vv.n
        @block.scalar
        def _(sc):
            sc.wait_ge(vv.h, pre)
            i2 = sc.activation(QRS[:, base:base + TT * nh],
                               QRS[:, base:base + TT * nh], AF.Sqrt)
            ss.n += 1
            i2.then_inc(ss.h, 1)
        sq_done = ss.n
        tvb2 = tv.n
        e0 = ev.n
        rcp_base = vv.n + 1
        with (nc.psum_tensor(f"tq{tag}{nh}a", [128, 512], BF16) as ta,
              nc.psum_tensor(f"tq{tag}{nh}b", [128, 512], BF16) as tb):
            tps = [ta, tb]

            def emit_scale(v, m):
                sl = m % 2
                if m >= 2:
                    v.wait_ge(tv.h, tvb2 + (m - 1) * (nh // 4))
                for h in range(nh):
                    ii = v.tensor_scalar_mul(
                        nr_t[:, sl * 2048 + h * 128: sl * 2048 + (h + 1) * 128],
                        src[:, m * ncol + h * 128: m * ncol + (h + 1) * 128],
                        QRS[:, base + m * nh + h: base + m * nh + h + 1])
                vv.n += 1
                ii.then_inc(vv.h, 1)

            def emit_ev(v, m):
                for g in range(nh // 4):
                    idx = m * (nh // 4) + g
                    v.wait_ge(tv.h, tvb2 + idx + 1)
                    ps = tps[idx % 2]
                    src2 = ps[:].rearrange("p (j f) -> p j f", f=128)
                    iv = dst_write(v, m, g, src2)
                    ev.n += 1
                    iv.then_inc(ev.h, 1)

            @block.vector
            def _(v):
                v.wait_ge(ss.h, sq_done)
                i3 = v.reciprocal(QRS[:, base:base + TT * nh],
                                  QRS[:, base:base + TT * nh])
                vv.n += 1
                i3.then_inc(vv.h, 1)
                v.wait_ge(vv.h, vv.n)
                for m in range(TT):
                    emit_scale(v, m)
                    if m >= 1:
                        emit_ev(v, m - 1)
                emit_ev(v, TT - 1)

            @block.tensor
            def _(t):
                for m in range(TT):
                    sl = m % 2
                    t.wait_ge(vv.h, rcp_base + m + 1)
                    for g in range(nh // 4):
                        idx = m * (nh // 4) + g
                        if idx >= 2:
                            t.wait_ge(ev.h, e0 + idx - 1)
                        ps = tps[idx % 2]
                        for j in range(4):
                            h = g * 4 + j
                            it = t.transpose(
                                ps[:, j * 128:(j + 1) * 128],
                                nr_t[:, sl * 2048 + h * 128:
                                     sl * 2048 + (h + 1) * 128], IDN[:])
                        tv.n += 1
                        it.then_inc(tv.h, 1)
        return (ev, ev.n)

    def k_write(v, m, g, src2):
        dst = knTm[:].rearrange("p (h f) -> p h f", f=TL)[
            :, g * 4:(g + 1) * 4, m * 128:(m + 1) * 128]
        return v.tensor_copy(dst, src2)

    def q_write(v, m, g, src2):
        dst = qnT[:].rearrange("p (h f) -> p h f", f=TL)[
            :, g * 4:(g + 1) * 4, m * 128:(m + 1) * 128]
        return v.tensor_copy(dst, src2)

    kdone = emit_rms(k_tm, 4, k_write, r2)
    g0, c0 = gq.n, cc.n
    @block.gpsimd
    def _(g):
        g.wait_ge(kdone[0].h, kdone[1])
        g.wait_ge(r3[0].h, r3[1])
        d = g.dma_start(out=p.t["kv_in"].ap()[0:1, 0:HKV * DH * TL]
                        .rearrange("o (q f) -> (o q) f", q=128), in_=knTm[:])
        gq.n += 16
        d.then_inc(gq.h, 16)
        d = g.dma_start(out=p.t["kv_in"].ap()[0:1, HKV * DH * TL:]
                        .rearrange("o (q f) -> (o q) f", q=128), in_=v_tm[:])
        gq.n += 16
        d.then_inc(gq.h, 16)
        g.wait_ge(gq.h, gq.n)
        icc = g.collective_compute(
            "AllGather", ALU.bypass,
            replica_groups=[[0, 1], [2, 3], [4, 5], [6, 7]],
            ins=[p.t["kv_in"].ap().opt()], outs=[p.t["kv_out"].ap().opt()])
        cc.n += 1
        icc.then_inc(cc.h, 1)
        g.wait_ge(cc.h, cc.n)
        for r in range(2):
            src = p.t["kv_out"].ap()[r:r + 1, 0:HKV * DH * TL].rearrange(
                "o (q h f) -> (o q) h f", q=128, h=HKV)
            dst = knT[:].rearrange("p (h f) -> p h f", f=2048)[
                :, :, r * TL:(r + 1) * TL]
            d = g.dma_start(out=dst, in_=src)
            gq.n += 16
            d.then_inc(gq.h, 16)
            src = p.t["kv_out"].ap()[r:r + 1, HKV * DH * TL:].rearrange(
                "o (q f) -> (o q) f", q=128)
            d = g.dma_start(out=v_full[:, r * 4096:(r + 1) * 4096], in_=src)
            gq.n += 16
            d.then_inc(gq.h, 16)
    kv_ready = gq.n

    qdone = emit_rms(q_tm, 16, q_write, r1)

    with (nc.psum_tensor(f"lpa{tag}", [128, TL], F32) as lpa,
          nc.psum_tensor(f"lpb{tag}", [128, TL], F32) as lpb,
          nc.psum_tensor(f"ops{tag}", [128, TL], F32) as ops,
          nc.psum_tensor(f"sps{tag}", [1, TL], F32) as sps):
        lps = [lpa, lpb]
        t0, s0, v0 = tv.n, ss.n, vv.n
        @block.tensor
        def _(t):
            t.wait_ge(gq.h, kv_ready)
            t.wait_ge(qdone[0].h, qdone[1])
            for h in range(H):
                hb_t = t0 + h * 33
                hb_s = s0 + h * 17
                hb_v = v0 + h * 2
                kvh = h // 4
                for kt in range(16):
                    if kt >= 2:
                        t.wait_ge(ss.h, hb_s + kt - 1)
                    if kt == 0 and h > 0:
                        t.wait_ge(vv.h, hb_v)
                    for qc in range(2):
                        it = t.matmul(
                            lps[kt % 2][:, qc * 512:(qc + 1) * 512],
                            knT[:, kvh * 2048 + kt * 128: kvh * 2048 + (kt + 1) * 128],
                            qnT[:, h * TL + qc * 512: h * TL + (qc + 1) * 512],
                            start=True, stop=True)
                    tv.n += 1
                    it.then_inc(tv.h, 1)
                    t.wait_ge(ss.h, hb_s + kt + 1)
                    for qc in range(2):
                        pslice = pT[:, kt * TL + qc * 512: kt * TL + (qc + 1) * 512]
                        iw = t.matmul(sps[:, qc * 512:(qc + 1) * 512],
                                      p.s["ONESC"][:], pslice,
                                      start=(kt == 0), stop=(kt == 15))
                        it2 = t.matmul(ops[:, qc * 512:(qc + 1) * 512],
                                       v_full[:, kt * 512 + kvh * 128:
                                              kt * 512 + (kvh + 1) * 128],
                                       pslice,
                                       start=(kt == 0), stop=(kt == 15))
                    tv.n += 1
                    it2.then_inc(tv.h, 1)
                t.wait_ge(vv.h, hb_v + 1)
                for qc in range(2):
                    ib = t.matmul(lps[0][:, qc * 512:(qc + 1) * 512], ONESR[:],
                                  rb_sb[0:1, qc * 512:(qc + 1) * 512],
                                  start=True, stop=True)
                tv.n += 1
                ib.then_inc(tv.h, 1)
        @block.scalar
        def _(sc):
            for h in range(H):
                hb_t = t0 + h * 33
                for kt in range(16):
                    sc.wait_ge(tv.h, hb_t + 2 * kt + 1)
                    ie = sc.activation(pT[:, kt * TL:(kt + 1) * TL],
                                       lps[kt % 2][:], AF.Exp,
                                       bias=np.float32(EXP_BIAS),
                                       scale=np.float32(RT128))
                    ss.n += 1
                    ie.then_inc(ss.h, 1)
                sc.wait_ge(tv.h, hb_t + 33)
                ir = sc.copy(rb_sb[:], lps[0][:])
                ss.n += 1
                ir.then_inc(ss.h, 1)
        @block.vector
        def _(v):
            for h in range(H):
                hb_t = t0 + h * 33
                hb_s = s0 + h * 17
                v.wait_ge(tv.h, hb_t + 32)
                with nc.allow_low_precision(reason="softmax denom bcast bf16"):
                    i1 = v.reciprocal(rb_sb[0:1, :], sps[:])
                vv.n += 1
                i1.then_inc(vv.h, 1)
                v.wait_ge(ss.h, hb_s + 17)
                i2 = v.tensor_tensor(aoT[:, h * TL:(h + 1) * TL], ops[:],
                                     rb_sb[:], ALU.mult)
                vv.n += 1
                i2.then_inc(vv.h, 1)
    heads_done = (vv, vv.n)

    def lhsT_ao(kc, m):
        return aoT[:, kc * TL + m * 128: kc * TL + (m + 1) * 128]

    def evac_wo(nch, m, bank, mm_done, evs):
        @block.vector
        def _(v):
            v.wait_ge(mm_done[0].h, mm_done[1])
            dst = X[:, m * C + nch * 512: m * C + (nch + 1) * 512]
            iv = v.tensor_tensor(dst, dst, bank[:], ALU.add)
            evs.n += 1
            iv.then_inc(evs.h, 1)

    r4 = emit_wstream_mm(p, f"wo{tag}", p.t["Wo_bf"].ap(), 2048, lhsT_ao,
                         evac_wo, cwo, t_ready=heads_done,
                         ready2=(tv, tv_after_v))
    return r4


def emit_w_quant_prep(p, x_ready):
    if "wq" in p.cast_done:
        return p.cast_done["wq"]
    nc, block = p.nc, p.block
    sq, gq, cc = p.sem("sq"), p.sem("gq"), p.sem("cc")
    vv, ss, tv, ev = p.sem("vv"), p.sem("ss"), p.sem("tv"), p.sem("ev")
    GAM, IDN, NB = p.s["GAM"], p.s["IDN"], p.s["NB"]
    wch = p.scra("qw_f", [128, 8192], F32, 0)
    wsg = p.scra("qw_s", [128, 8192], BF16, 32768)
    wng = p.scra("qw_n", [128, 8192], BF16, 49152)

    chunks = [("w1Ts", "w1s_sl", 0, 8192), ("w1Ts", "w1s_sl", 1, 8192)] + \
             [("w2Ts", "w2s_sl", i, 2048) for i in range(8)]
    sq0, vv0 = sq.n, vv.n
    @block.sync
    def _(sy):
        sy.wait_ge(x_ready[0].h, x_ready[1])
        for i, (nm, _, r, w) in enumerate(chunks):
            if i >= 1:
                sy.wait_ge(vv.h, vv0 + i)
            d = sy.dma_start(out=wch[:, 0:w],
                             in_=p.t[nm].ap()[r * 128:(r + 1) * 128, :])
            sq.n += 16
            d.then_inc(sq.h, 16)
    @block.vector
    def _(v):
        for i, (nm, _, r, w) in enumerate(chunks):
            v.wait_ge(sq.h, sq0 + 16 * (i + 1))
            iv = v.reduce_sum(NB[:, 10 + i:11 + i], wch[:, 0:w], axis=AXX,
                              apply_absolute_value=True)
            vv.n += 1
            iv.then_inc(vv.h, 1)
        v.wait_ge(vv.h, vv.n)
        i1 = v.reduce_sum(NB[:, 8:9], NB[:, 10:12], axis=AXX)
        i2 = v.reduce_sum(NB[:, 9:10], NB[:, 12:20], axis=AXX)
        vv.n += 1
        i2.then_inc(vv.h, 1)
    part_done = vv.n
    with nc.psum_tensor("qwtp", [2, 128], F32) as tp:
        t0 = tv.n
        @block.tensor
        def _(t):
            t.wait_ge(vv.h, part_done)
            it = t.transpose(tp[:], NB[:, 8:10], p.s["IDNF"][:])
            tv.n += 1
            it.then_inc(tv.h, 1)
        @block.vector
        def _(v):
            v.wait_ge(tv.h, t0 + 1)
            v.memset(NB[:, 20:21], 0.0)
            v.wait_ge(vv.h, vv.n)
            iv = v.reduce_sum(NB[0:2, 20:21], tp[:], axis=AXX)
            vv.n += 1
            iv.then_inc(vv.h, 1)
    red_done = vv.n
    @block.gpsimd
    def _(g):
        g.wait_ge(vv.h, red_done)
        d = g.dma_start(out=p.t["g_in"].ap(), in_=NB[0:8, 20:21])
        gq.n += 16
        d.then_inc(gq.h, 16)
        g.wait_ge(gq.h, gq.n)
        icc = g.collective_compute(
            "AllReduce", ALU.add, replica_groups=[list(range(8))],
            ins=[p.t["g_in"].ap().opt()], outs=[p.t["g_out"].ap().opt()])
        cc.n += 1
        icc.then_inc(cc.h, 1)
        g.wait_ge(cc.h, cc.n)
        d = g.dma_start(out=GAM[:, 0:2],
                        in_=p.t["g_out"].ap()[0:1, 0:2].broadcast_to((128, 2)))
        gq.n += 16
        d.then_inc(gq.h, 16)
    gam_loaded = gq.n
    @block.vector
    def _(v):
        v.wait_ge(gq.h, gam_loaded)
        i1 = v.tensor_scalar(GAM[:, 0:2], GAM[:, 0:2], 1.0 / (HF * C), EPS_W,
                             ALU.mult, ALU.add)
        p.vfence(v, i1)
        i2 = v.tensor_single_scalar(GAM[:, 2:4], GAM[:, 0:2], 0.5, ALU.mult)
        p.vfence(v, i2)
        i3 = v.tensor_single_scalar(GAM[:, 4:6], GAM[:, 2:4], -1.0, ALU.mult)
        vv.n += 1
        i3.then_inc(vv.h, 1)
    gam_done = vv.n

    sq1, vv1, gq1 = sq.n, vv.n, gq.n
    @block.sync
    def _(sy):
        sy.wait_ge(vv.h, part_done)
        for i, (nm, _, r, w) in enumerate(chunks):
            if i >= 1:
                sy.wait_ge(vv.h, vv1 + i)
            d = sy.dma_start(out=wch[:, 0:w],
                             in_=p.t[nm].ap()[r * 128:(r + 1) * 128, :])
            sq.n += 16
            d.then_inc(sq.h, 16)
    @block.vector
    def _(v):
        v.wait_ge(vv.h, gam_done)
        for i, (nm, dst, r, w) in enumerate(chunks):
            v.wait_ge(sq.h, sq1 + 16 * (i + 1))
            if i >= 1:
                v.wait_ge(gq.h, gq1 + 16 * i)
            col = 2 if nm == "w1Ts" else 3
            i1 = v.tensor_single_scalar(wsg[:, 0:w], wch[:, 0:w],
                                        GAM[:, col:col + 1], ALU.is_gt)
            i2 = v.tensor_single_scalar(wng[:, 0:w], wch[:, 0:w],
                                        GAM[:, col + 2:col + 3], ALU.is_lt)
            p.vfence(v, i2)
            i3 = v.tensor_tensor(wsg[:, 0:w], wsg[:, 0:w], wng[:, 0:w],
                                 ALU.subtract)
            vv.n += 1
            i3.then_inc(vv.h, 1)
    @block.gpsimd
    def _(g):
        for i, (nm, dst, r, w) in enumerate(chunks):
            g.wait_ge(vv.h, vv1 + i + 1)
            d = g.dma_start(out=p.t[dst].ap()[r * 128:(r + 1) * 128, :],
                            in_=wsg[:, 0:w])
            gq.n += 16
            d.then_inc(gq.h, 16)
        g.wait_ge(gq.h, gq.n)
        for nm_in, nm_out in [("w1s_sl", "w1sT"), ("w2s_sl", "w2sT")]:
            icc = g.collective_compute(
                "AllGather", ALU.bypass, replica_groups=[list(range(8))],
                ins=[p.t[nm_in].ap().opt()], outs=[p.t[nm_out].ap().opt()])
            cc.n += 1
            icc.then_inc(cc.h, 1)
        g.wait_ge(cc.h, cc.n)
    p.cast_done["wq"] = (cc, cc.n)
    return p.cast_done["wq"]


def emit_l1ffn(p, tag, x_ready):
    nc, block = p.nc, p.block
    sq, gq, cc = p.sem("sq"), p.sem("gq"), p.sem("cc")
    vv, ss, tv, ev = p.sem("vv"), p.sem("ss"), p.sem("tv"), p.sem("ev")
    X, XLNT, IDN, ONESR = p.s["X"], p.s["XLNT"], p.s["IDN"], p.s["ONESR"]
    S1, FT, NB, GAM = p.s["S1"], p.s["FT"], p.s["NB"], p.s["GAM"]

    xlnt_done = emit_ln_to_xlnt(p, x_ready, tag, quant=True)
    wq_done = emit_w_quant_prep(p, x_ready)
    if L1FFN_STOP <= 1:
        return wq_done

    rT = p.at(f"rT{tag}", [128, HFT * TL], BF16, OFF_X)
    w1b = p.scra(f"w1b{tag}", [128, 2 * CT * 128], BF16, 65536)
    mx = p.scra(f"mx{tag}", [128, TL], BF16, 73728)
    cb16 = p.scra(f"cb{tag}", [128, 8], BF16, 75776)
    crow = p.scra(f"cr{tag}", [8, 128], BF16, 76800)
    crf = p.scra(f"crf{tag}", [1, TL], BF16, 77824)
    c_bc = p.scra(f"cbc{tag}", [128, TL], BF16, 79872)
    usq = p.scra(f"usq{tag}", [128, 2 * TL], BF16, 81920)
    sqf = p.scra(f"sqf2{tag}", [128, 2 * TL], F32, 90112)
    k2i = p.scra(f"k2i2{tag}", [128, 2 * TL], I16, 86016)
    out2c = p.scra(f"o2c{tag}", [128, TT * 512], F32, 67584)
    o2sb = p.scra(f"o2s{tag}", [128, 4 * C], F32, 65536)
    w2b = p.scra(f"w2b{tag}", [128, 2 * 512], BF16, 65536)

    ln_scalar_done = (p.sem("ss"), p.sem("ss").n)
    g0 = gq.n
    @block.gpsimd
    def _(g):
        g.wait_ge(ss.h, ln_scalar_done[1])
        g.wait_ge(xlnt_done[0].h, xlnt_done[1])
        d = g.dma_start(out=p.t["xspill"].ap().rearrange("(m q) c -> q m c", q=128),
                        in_=X[:].rearrange("p (m c) -> p m c", c=C))
        gq.n += 16
        d.then_inc(gq.h, 16)
    spill_done = gq.n

    sq2, tv2, ss2, vv2 = sq.n, tv.n, ss.n, vv.n
    with (nc.psum_tensor(f"h{tag}a", [128, TL], F32) as ha,
          nc.psum_tensor(f"h{tag}b", [128, TL], F32) as hb):
        hps = [ha, hb]
        @block.sync
        def _(sy):
            sy.wait_ge(wq_done[0].h, wq_done[1])
            sy.wait_ge(spill_done and gq.h, spill_done)
            for ot in range(HFT):
                if ot >= 2:
                    sy.wait_ge(tv.h, tv2 + ot - 1)
                d = sy.dma_start(
                    out=w1b[:, (ot % 2) * CT * 128:(ot % 2 + 1) * CT * 128],
                    in_=p.t["w1sT"].ap()[:, ot * 128:(ot + 1) * 128]
                        .rearrange("(k q) f -> q k f", q=128))
                sq.n += 16
                d.then_inc(sq.h, 16)
        @block.tensor
        def _(t):
            t.wait_ge(xlnt_done[0].h, xlnt_done[1])
            for ot in range(HFT):
                t.wait_ge(sq.h, sq2 + 16 * (ot + 1))
                if ot >= 2:
                    t.wait_ge(ss.h, ss2 + ot - 1)
                ps = hps[ot % 2]
                wj = w1b[:, (ot % 2) * CT * 128:(ot % 2 + 1) * CT * 128]
                for kc in range(CT):
                    for qc in range(2):
                        it = t.matmul(
                            ps[:, qc * 512:(qc + 1) * 512],
                            wj[:, kc * 128:(kc + 1) * 128],
                            XLNT[:, kc * TL + qc * 512: kc * TL + (qc + 1) * 512],
                            start=(kc == 0), stop=(kc == CT - 1))
                tv.n += 1
                it.then_inc(tv.h, 1)
        @block.scalar
        def _(sc):
            sc.wait_ge(gq.h, spill_done)
            for ot in range(HFT):
                sc.wait_ge(tv.h, tv2 + ot + 1)
                ic = sc.activation(rT[:, ot * TL:(ot + 1) * TL], hps[ot % 2][:],
                                   AF.Relu)
                ss.n += 1
                ic.then_inc(ss.h, 1)
        @block.vector
        def _(v):
            for ot in range(HFT):
                v.wait_ge(ss.h, ss2 + ot + 1)
                if ot == 0:
                    iv = v.tensor_copy(mx[:], rT[:, 0:TL])
                else:
                    iv = v.tensor_max(mx[:], mx[:], rT[:, ot * TL:(ot + 1) * TL])
                vv.n += 1
                iv.then_inc(vv.h, 1)
                v.wait_ge(vv.h, vv.n)
    mx_done = vv.n
    if L1FFN_STOP <= 2:
        return (vv, mx_done)

    with (nc.psum_tensor(f"mt{tag}a", [128, 128], BF16) as mta,
          nc.psum_tensor(f"mt{tag}b", [128, 128], BF16) as mtb):
        mtps = [mta, mtb]
        t3, v3 = tv.n, vv.n
        @block.tensor
        def _(t):
            t.wait_ge(vv.h, mx_done)
            for j in range(TT):
                if j >= 2:
                    t.wait_ge(vv.h, v3 + j - 1)
                it = t.transpose(mtps[j % 2][:], mx[:, j * 128:(j + 1) * 128],
                                 IDN[:])
                tv.n += 1
                it.then_inc(tv.h, 1)
        @block.vector
        def _(v):
            for j in range(TT):
                v.wait_ge(tv.h, t3 + j + 1)
                iv = v.reduce_max(NB[:, j:j + 1], mtps[j % 2][:], axis=AXX)
                vv.n += 1
                iv.then_inc(vv.h, 1)
    mt_red = vv.n

    @block.vector
    def _(v):
        v.wait_ge(vv.h, mt_red)
        i0 = v.tensor_scalar_max(NB[:, 0:TT], NB[:, 0:TT], 1e-30)
        p.vfence(v, i0)
        i1 = v.reciprocal(S1[:, 16:16 + TT], S1[:, 0:TT])
        p.vfence(v, i1)
        i2 = v.tensor_tensor(FT[:, 0:TT], NB[:, 0:TT], S1[:, 16:16 + TT],
                             ALU.mult)
        p.vfence(v, i2)
        i3 = v.tensor_tensor(FT[:, 0:TT], FT[:, 0:TT], FT[:, 0:TT], ALU.mult)
        p.vfence(v, i3)
        i4 = v.tensor_tensor(GAM[:, 6:7], GAM[:, 0:1], GAM[:, 0:1], ALU.mult)
        p.vfence(v, i4)
        i5 = v.tensor_tensor(GAM[:, 7:8], GAM[:, 6:7], GAM[:, 1:2], ALU.mult)
        p.vfence(v, i5)
        i6 = v.tensor_single_scalar(GAM[:, 6:7], GAM[:, 7:8], 0.5 / QMAX,
                                    ALU.mult)
        p.vfence(v, i6)
        i7 = v.tensor_scalar_mul(FT[:, 0:TT], FT[:, 0:TT], GAM[:, 6:7])
        p.vfence(v, i7)
        i8 = v.reciprocal(NB[:, 8:8 + TT], NB[:, 0:TT])
        p.vfence(v, i8)
        i9 = v.tensor_single_scalar(cb16[:, 0:TT], NB[:, 8:8 + TT],
                                    float(np.sqrt(QMAX)), ALU.mult)
        vv.n += 1
        i9.then_inc(vv.h, 1)
    ft_done = vv.n

    with (nc.psum_tensor(f"ct{tag}", [8, 128], BF16) as ctp,
          nc.psum_tensor(f"cb{tag}", [128, 512], F32) as cbp):
        t4 = tv.n
        @block.tensor
        def _(t):
            t.wait_ge(vv.h, ft_done)
            it = t.transpose(ctp[:], cb16[:, 0:TT], IDN[:])
            tv.n += 1
            it.then_inc(tv.h, 1)
        s4 = ss.n
        @block.scalar
        def _(sc):
            sc.wait_ge(tv.h, t4 + 1)
            ic = sc.copy(crow[:], ctp[:])
            ss.n += 1
            ic.then_inc(ss.h, 1)
        g4 = gq.n
        @block.gpsimd
        def _(g):
            g.wait_ge(ss.h, s4 + 1)
            d = g.dma_start(out=crf[:].rearrange("o (a f) -> o a f", a=8), in_=crow[:])
            gq.n += 16
            d.then_inc(gq.h, 16)
        t5 = tv.n
        s5 = ss.n
        @block.tensor
        def _(t):
            t.wait_ge(gq.h, g4 + 16)
            for qc in range(2):
                if qc >= 1:
                    t.wait_ge(ss.h, s5 + qc)
                it = t.matmul(cbp[:], ONESR[:], crf[:, qc * 512:(qc + 1) * 512],
                              start=True, stop=True)
                tv.n += 1
                it.then_inc(tv.h, 1)
        @block.scalar
        def _(sc):
            for qc in range(2):
                sc.wait_ge(tv.h, t5 + qc + 1)
                ic = sc.copy(c_bc[:, qc * 512:(qc + 1) * 512], cbp[:])
                ss.n += 1
                ic.then_inc(ss.h, 1)
    cbc_done = (ss, ss.n)
    if L1FFN_STOP <= 3:
        return cbc_done

    v6, s6 = vv.n, ss.n

    def emit_k2wb(v, j):
        slj = j % 2
        ik = v.tensor_single_scalar(k2i[:, slj * TL:(slj + 1) * TL],
                                    sqf[:, slj * TL:(slj + 1) * TL],
                                    128.0, ALU.add)
        vv.n += 1
        ik.then_inc(vv.h, 1)
        v.wait_ge(vv.h, vv.n)
        iv2 = v.tensor_scalar(rT[:, j * TL:(j + 1) * TL],
                              k2i[:, slj * TL:(slj + 1) * TL],
                              128.0, QMAX, ALU.subtract, ALU.min)
        vv.n += 1
        iv2.then_inc(vv.h, 1)

    @block.vector
    def _(v):
        v.wait_ge(cbc_done[0].h, cbc_done[1])
        for ot in range(HFT):
            sl = ot % 2
            if ot >= 2:
                v.wait_ge(ss.h, s6 + ot - 1)
            i1 = v.tensor_tensor(usq[:, sl * TL:(sl + 1) * TL],
                                 rT[:, ot * TL:(ot + 1) * TL], c_bc[:], ALU.mult)
            vv.n += 1
            i1.then_inc(vv.h, 1)
            if ot >= 1:
                v.wait_ge(ss.h, s6 + ot)
                emit_k2wb(v, ot - 1)
        v.wait_ge(ss.h, s6 + HFT)
        emit_k2wb(v, HFT - 1)
    @block.scalar
    def _(sc):
        for ot in range(HFT):
            sl = ot % 2
            sc.wait_ge(vv.h, v6 + (1 if ot == 0 else 3 * ot - 1))
            ic = sc.activation(sqf[:, sl * TL:(sl + 1) * TL],
                               usq[:, sl * TL:(sl + 1) * TL], AF.Square)
            ss.n += 1
            ic.then_inc(ss.h, 1)
    k2_done = vv.n
    if L1FFN_STOP <= 4:
        return (vv, k2_done)

    sq7, tv7, ss7, gq7 = sq.n, tv.n, ss.n, gq.n
    with contextlib.ExitStack() as pes:
        banks = [pes.enter_context(
            nc.psum_tensor(f"o2{tag}{i}", [128, 512], F32)) for i in range(8)]
        @block.sync
        def _(sy):
            sy.wait_ge(vv.h, k2_done)
            for cc2 in range(4):
                for kt in range(HFT):
                    idx = cc2 * HFT + kt
                    if idx >= 2:
                        sy.wait_ge(tv.h, tv7 + idx - 1)
                    d = sy.dma_start(
                        out=w2b[:, (idx % 2) * 512:(idx % 2 + 1) * 512],
                        in_=p.t["w2sT"].ap()[kt * 128:(kt + 1) * 128,
                                             cc2 * 512:(cc2 + 1) * 512])
                    sq.n += 16
                    d.then_inc(sq.h, 16)
        @block.tensor
        def _(t):
            for cc2 in range(4):
                for kt in range(HFT):
                    idx = cc2 * HFT + kt
                    t.wait_ge(sq.h, sq7 + 16 * (idx + 1))
                    if kt == 0 and cc2 > 0:
                        t.wait_ge(ss.h, ss7 + cc2 * TT)
                    wap = w2b[:, (idx % 2) * 512:(idx % 2 + 1) * 512]
                    for m in range(TT):
                        it = t.matmul(banks[m][:],
                                      rT[:, kt * TL + m * 128: kt * TL + (m + 1) * 128],
                                      wap, start=(kt == 0), stop=(kt == HFT - 1))
                    tv.n += 1
                    it.then_inc(tv.h, 1)
        @block.scalar
        def _(sc):
            for cc2 in range(4):
                sc.wait_ge(tv.h, tv7 + (cc2 + 1) * HFT)
                if cc2 >= 1:
                    sc.wait_ge(gq.h, gq7 + 16 * cc2)
                for m in range(TT):
                    ic = sc.activation(out2c[:, m * 512:(m + 1) * 512],
                                       banks[m][:], AF.Identity,
                                       scale=FT[:, m:m + 1])
                    ss.n += 1
                    ic.then_inc(ss.h, 1)
        @block.gpsimd
        def _(g):
            for cc2 in range(4):
                g.wait_ge(ss.h, ss7 + (cc2 + 1) * TT)
                src = out2c[:].rearrange("p (m f) -> p m f", f=512)
                dst = p.t["out2d"].ap().rearrange(
                    "(m q) c -> q m c", q=128)[:, :, cc2 * 512:(cc2 + 1) * 512]
                d = g.dma_start(out=dst, in_=src)
                gq.n += 16
                d.then_inc(gq.h, 16)
    mm2_done = gq.n
    if L1FFN_STOP <= 5:
        @block.gpsimd
        def _(g):
            g.wait_ge(gq.h, mm2_done)
            d = g.dma_start(out=X[:].rearrange("p (m c) -> p m c", c=C),
                            in_=p.t["xspill"].ap().rearrange("(m q) c -> q m c", q=128))
            gq.n += 16
            d.then_inc(gq.h, 16)
        return (p.sem("gq"), gq.n)

    if DEBUG_DUMP == "l1ffn":
        @block.gpsimd
        def _(g):
            g.wait_ge(gq.h, mm2_done)
            for nm, src, w in [("GAM", GAM[:, 0:8], 8), ("S1", S1[:, 0:8], 8),
                               ("NB", NB[:, 0:16], 16), ("FT", FT[:, 0:8], 8),
                               ("cbc", p.s["QRS"][:, 0:1], 1)]:
                pass
            d = g.dma_start(out=p.t["dbg"].ap()[:, 0:8], in_=GAM[:, 0:8])
            gq.n += 16; d.then_inc(gq.h, 16)
            d = g.dma_start(out=p.t["dbg"].ap()[:, 8:16], in_=S1[:, 0:8])
            gq.n += 16; d.then_inc(gq.h, 16)
            d = g.dma_start(out=p.t["dbg"].ap()[:, 16:32], in_=NB[:, 0:16])
            gq.n += 16; d.then_inc(gq.h, 16)
            d = g.dma_start(out=p.t["dbg"].ap()[:, 32:40], in_=FT[:, 0:8])
            gq.n += 16; d.then_inc(gq.h, 16)
            d = g.dma_start(out=p.t["dbg"].ap()[:, 40:1064], in_=c_bc[:])
            gq.n += 16; d.then_inc(gq.h, 16)
            d = g.dma_start(out=p.t["dbg"].ap()[:, 1064:1080],
                            in_=rT[:, 0:16])
            gq.n += 16; d.then_inc(gq.h, 16)
            d = g.dma_start(out=p.t["dbg"].ap()[:, 1080:1088],
                            in_=cb16[:, 0:8])
            gq.n += 16; d.then_inc(gq.h, 16)
            d = g.dma_start(out=p.t["dbg"].ap()[0:8, 1088:1216],
                            in_=crow[:])
            gq.n += 16; d.then_inc(gq.h, 16)
            d = g.dma_start(out=p.t["dbg"].ap()[0:1, 1216:2040],
                            in_=crf[0:1, 0:824])
            gq.n += 16; d.then_inc(gq.h, 16)

    g8 = gq.n
    @block.gpsimd
    def _(g):
        g.wait_ge(gq.h, mm2_done)
        d = g.dma_start(out=X[:].rearrange("p (m c) -> p m c", c=C),
                        in_=p.t["xspill"].ap().rearrange("(m q) c -> q m c", q=128))
        gq.n += 16
        d.then_inc(gq.h, 16)
        d = g.dma_start(out=o2sb[:, 0:C * 2].rearrange("p (m c) -> p m c", c=C),
                        in_=p.t["out2d"].ap().rearrange("(m q) c -> q m c", q=128)[:, 0:2, :])
        gq.n += 16
        d.then_inc(gq.h, 16)
    v8 = vv.n
    @block.gpsimd
    def _(g):
        for ch in range(1, 4):
            g.wait_ge(vv.h, v8 + 2 * (ch - 1) + 1)
            d = g.dma_start(
                out=o2sb[:, (ch % 2) * 2 * C:(ch % 2) * 2 * C + 2 * C]
                    .rearrange("p (m c) -> p m c", c=C),
                in_=p.t["out2d"].ap().rearrange("(m q) c -> q m c", q=128)[
                    :, 2 * ch:2 * ch + 2, :])
            gq.n += 16
            d.then_inc(gq.h, 16)
    @block.vector
    def _(v):
        v.wait_ge(gq.h, g8 + 32)
        for ch in range(4):
            if ch >= 1:
                v.wait_ge(gq.h, g8 + 32 + 16 * ch)
            for mm in range(2):
                m = ch * 2 + mm
                iv = v.tensor_tensor(X[:, m * C:(m + 1) * C],
                                     X[:, m * C:(m + 1) * C],
                                     o2sb[:, (ch % 2) * 2 * C + mm * C:
                                          (ch % 2) * 2 * C + (mm + 1) * C],
                                     ALU.add)
                vv.n += 1
                iv.then_inc(vv.h, 1)
    return (vv, vv.n)


def build_program():
    nc = bass.Bass("TRN2", target_bir_lowering=False, detect_race_conditions=False)
    es = contextlib.ExitStack()
    tens = {}

    def dti(name, shape, dtype, kind="Internal"):
        tens[name] = nc.dram_tensor(name, shape, dtype, kind=kind)

    dti("x", [TL, C], F32, "ExternalInput")
    dti("Wq", [C, H * DH], F32, "ExternalInput")
    dti("Wk", [C, HKV * DH], F32, "ExternalInput")
    dti("Wv", [C, HKV * DH], F32, "ExternalInput")
    dti("Wo", [H * DH, C], F32, "ExternalInput")
    dti("gate_w", [C, C], F32, "ExternalInput")
    dti("agg_w", [C, C], F32, "ExternalInput")
    dti("bc_w", [C, C], F32, "ExternalInput")
    dti("d1s", [C, 1024], F32, "ExternalInput")
    dti("d2s", [1024, C], F32, "ExternalInput")
    dti("d1bs", [1, 1024], F32, "ExternalInput")
    dti("L1W", [C, 3 * C], F32, "ExternalInput")
    dti("L1bT", [128, 48], F32, "ExternalInput")
    dti("w1Ts", [256, HF], F32, "ExternalInput")
    dti("w2Ts", [1024, C], F32, "ExternalInput")
    dti("bsel", [4, 1], F32, "ExternalInput")
    for nm in ["in_g", "in_b", "pn_g", "pn_b", "out_g", "out_b",
               "agg_b", "bc_b", "gate_b", "d2_b"]:
        dti(nm, [1, C], F32, "ExternalInput")
    dti("out", [TL, C], F32, "ExternalOutput")
    dti("dbg", [128, 2048], F32, "ExternalOutput")
    for nm, shp in [("gate_bf", [C, C]), ("agg_bf", [C, C]), ("bc_bf", [C, C]),
                    ("Wq_bf", [C, H * DH]), ("Wk_bf", [C, HKV * DH]),
                    ("Wv_bf", [C, HKV * DH]), ("Wo_bf", [H * DH, C]),
                    ("L1W_bf", [C, 3 * C]), ("d1s_bf", [C, 1024]),
                    ("d2s_bf", [1024, C]),
                    ("w1s_sl", [256, HF]), ("w2s_sl", [1024, C]),
                    ("w1sT", [C, HF]), ("w2sT", [HF, C])]:
        dti(nm, shp, BF16)
    dti("g_in", [1, 8], F32)
    dti("g_out", [1, 8], F32)
    dti("gsc", [1, 4], F32)
    dti("th_in", [1, C], BF16)
    dti("th_out", [8, C], BF16)
    dti("ar_in", [4, C], F32)
    dti("ar_out", [4, C], F32)
    dti("kv_in", [1, KVE], BF16)
    dti("kv_out", [2, KVE], BF16)
    dti("st_in", [1, 3 * C], F32)
    dti("st_out", [2, 3 * C], F32)
    dti("bcd", [1, C], F32)
    dti("xspill", [TL, C], F32)
    dti("out2d", [TL, C], F32)

    at = nc.alloc_sbuf_tensor_at
    sb = {}
    sb["X"] = at("X", [128, TT * C], F32, offset=OFF_X)
    sb["XLNT"] = at("XLNT", [128, CT * TL], BF16, offset=OFF_XLNT)
    mo = OFF_MISC
    def msb(name, shape, dtype):
        nonlocal mo
        sb[name] = at(name, shape, dtype, offset=mo)
        sz = int(np.prod(shape[1:])) * mybir.dt.size(dtype)
        mo += (sz + 31) // 32 * 32
    msb("MU", [128, 32], F32)
    msb("RS", [128, 32], F32)
    msb("NB", [128, 32], F32)
    msb("S1", [128, 32], F32)
    msb("FT", [128, 32], F32)
    msb("BNS", [128, 24], F32)
    msb("MV", [128, 2], F32)
    msb("IDN", [128, 128], BF16)
    msb("ONESR", [1, 128], BF16)
    msb("ONESC", [128, 1], BF16)
    msb("IDNF", [128, 128], F32)
    msb("GAM", [128, 8], F32)
    msb("QRS", [128, 256], F32)
    msb("STAT", [128, 144], F32)
    msb("BT", [128, 48], F32)
    assert mo <= 229376, mo

    with nc.Block() as block:
        p = P(nc, es, block)
        p.t = tens
        p.s = sb
        emit_all(p)
    es.close()
    for alloc in nc.m.functions[0].allocations:
        mls = getattr(alloc, "memorylocations", None) or []
        for ml in mls:
            if ml.type == "SB" and ml.name.startswith("const-"):
                assert int(ml.addr) + 64 <= OFF_X, (ml.name, ml.addr)
    return nc


def emit_all(p):
    nc, block = p.nc, p.block
    sq, gq = p.sem("sq"), p.sem("gq")
    vv, gg = p.sem("vv"), p.sem("gg")
    X = p.s["X"]

    @block.sync
    def _(sy):
        xr = p.t["x"].ap().rearrange("(m q) c -> q m c", q=128)
        xd = X[:].rearrange("p (m c) -> p m c", c=C)
        d = sy.dma_start(out=xd, in_=xr)
        sq.n += 16
        d.then_inc(sq.h, 16)
        sy.wait_ge(sq.h, sq.n)
    x_ready = (sq, sq.n)

    ii = p.scra("iotai", [128, 128], I32, 69632)
    @block.gpsimd
    def _(g):
        gg.inc(g.iota(ii[:], pattern=[[1, 128]], base=0, channel_multiplier=-1))
    @block.vector
    def _(v):
        v.wait_ge(gg.h, gg.n)
        v.tensor_single_scalar(p.s["IDN"][:], ii[:], 0, ALU.is_equal)
        v.tensor_single_scalar(p.s["IDNF"][:], ii[:], 0, ALU.is_equal)
        v.memset(p.s["ONESR"][:], 1.0)
        vv.inc(v.memset(p.s["ONESC"][:], 1.0))

    layers = [
        ("f0a", emit_l0ffn), ("a0", emit_l0attn), ("f0b", emit_l0ffn),
        ("f1a", emit_l1ffn), ("a1", emit_l1attn), ("f1b", emit_l1ffn),
    ]
    run = RUN_LIST if RUN_LIST is not None else list(range(N_SUBLAYERS))
    for i in run:
        tag, fn = layers[i]
        x_ready = fn(p, tag, x_ready)

    @block.gpsimd
    def _(g):
        g.wait_ge(x_ready[0].h, x_ready[1])
        xr = p.t["out"].ap().rearrange("(m q) c -> q m c", q=128)
        d = g.dma_start(out=xr, in_=X[:].rearrange("p (m c) -> p m c", c=C))
        gq.n += 16
        d.then_inc(gq.h, 16)
        g.wait_ge(gq.h, gq.n)


def kernel_dbg(**inputs):
    outs = kernel(**inputs, _want_dbg=True)
    return outs


def kernel(_want_dbg=False, **inputs):
    nc = build_program()
    w1T = np.ascontiguousarray(np.transpose(inputs["L1_ffn_w1"]))
    w2T = np.ascontiguousarray(np.transpose(inputs["L1_ffn_w2"]))
    Wq = np.ascontiguousarray(inputs["L0_attn_Wq"].reshape(C, H * DH))
    Wk = np.ascontiguousarray(inputs["L0_attn_Wk"].reshape(C, HKV * DH))
    Wv = np.ascontiguousarray(inputs["L0_attn_Wv"].reshape(C, HKV * DH))
    bT = np.ascontiguousarray(inputs["L1_attn_b"].reshape(48, 128).T)
    x = inputs["L0_attn_x_q"]
    row = lambda a: np.ascontiguousarray(a.reshape(1, -1))
    in_maps = []
    for c in range(NCORES):
        b, hf = c // 2, c % 2
        bsel = np.zeros((4, 1), np.float32)
        bsel[b, 0] = 1.0
        im = {
            "x": np.ascontiguousarray(x[b, hf * TL:(hf + 1) * TL, :]),
            "Wq": Wq, "Wk": Wk, "Wv": Wv,
            "Wo": inputs["L0_attn_Wo"],
            "gate_w": inputs["L0_ffn_gate_w"],
            "agg_w": inputs["L0_ffn_agg_w"],
            "bc_w": inputs["L0_ffn_bc_w"],
            "d1s": np.ascontiguousarray(
                inputs["L0_ffn_d1_w"][:, c * 1024:(c + 1) * 1024]),
            "d2s": np.ascontiguousarray(
                inputs["L0_ffn_d2_w"][c * 1024:(c + 1) * 1024, :]),
            "d1bs": row(inputs["L0_ffn_d1_b"][c * 1024:(c + 1) * 1024]),
            "L1W": inputs["L1_attn_W"],
            "L1bT": bT,
            "w1Ts": np.ascontiguousarray(w1T[c * 256:(c + 1) * 256, :]),
            "w2Ts": np.ascontiguousarray(w2T[c * 1024:(c + 1) * 1024, :]),
            "bsel": bsel,
            "in_g": row(inputs["L0_ffn_in_gamma"]),
            "in_b": row(inputs["L0_ffn_in_beta"]),
            "pn_g": row(inputs["L0_ffn_pn_gamma"]),
            "pn_b": row(inputs["L0_ffn_pn_beta"]),
            "out_g": row(inputs["L0_ffn_out_gamma"]),
            "out_b": row(inputs["L0_ffn_out_beta"]),
            "agg_b": row(inputs["L0_ffn_agg_b"]),
            "bc_b": row(inputs["L0_ffn_bc_b"]),
            "gate_b": row(inputs["L0_ffn_gate_b"]),
            "d2_b": row(inputs["L0_ffn_d2_b"]),
        }
        in_maps.append({k: np.asarray(v) for k, v in im.items()})
    res = run_bass_kernel_spmd(nc, in_maps, core_ids=list(range(NCORES)))
    out = np.empty((B, T, C), np.float32)
    for c in range(NCORES):
        b, hf = c // 2, c % 2
        out[b, hf * TL:(hf + 1) * TL, :] = res.results[c]["out"]
    if _want_dbg:
        return out, [res.results[c].get("dbg") for c in range(NCORES)]
    return out
